# version 01
# baseline (speedup 1.0000x reference)
import numpy as np
import ml_dtypes
import concourse.bass as bass
import concourse.mybir as mybir
from concourse.bass_utils import run_bass_kernel_spmd

F32 = mybir.dt.float32
BF16 = mybir.dt.bfloat16
I32 = mybir.dt.int32
AF = mybir.ActivationFunctionType
ALU = mybir.AluOpType
AX = mybir.AxisListType
BFNP = ml_dtypes.bfloat16

P = 128
D = 1024
FC = D // P
T = 2048
TA = 4096
I3 = 3072
IC = I3 // P
TCW = 512
NT = T // TCW
NTA = TA // TCW
NH = 8
DK = 64
QMAX = 32767.0
FLOOR = 32767.0e-12
MAGIC = 12582912.0

_PHASES = "full"


class Sem:
    def __init__(self, nc, name):
        self.h = nc.alloc_semaphore(name)
        self.n = 0
        self.dma = False


class Prog:
    ENGINES = ("sync", "tensor", "vector", "scalar", "gpsimd")

    def __init__(self, nc):
        self.nc = nc
        self.q = {e: [] for e in self.ENGINES}

    def emit(self, e, fn):
        self.q[e].append(fn)

    def wait(self, e, snap):
        sem, val = snap
        assert val <= sem.n
        if sem.dma:
            val = sem.n
        h = sem.h
        self.emit(e, lambda eng: eng.wait_ge(h, val))

    def dma(self, e, sem, out, in_):
        h = sem.h
        sem.dma = True
        self.emit(e, lambda eng: eng.dma_start(out=out, in_=in_).then_inc(h, 16))
        sem.n += 16
        return (sem, sem.n)

    def inst(self, e, sem, fn):
        if sem is None:
            self.emit(e, fn)
            return None
        h = sem.h
        self.emit(e, lambda eng: fn(eng).then_inc(h, 1))
        sem.n += 1
        return (sem, sem.n)


def build_program(phases=_PHASES):
    nc = bass.Bass("TRN2", target_bir_lowering=False, debug=False, detect_race_conditions=False)

    x_in = nc.dram_tensor("x_in", [D, T], F32, kind="ExternalInput").ap()
    wqkvT = nc.dram_tensor("wqkvT", [D, 1536], F32, kind="ExternalInput").ap()
    woT4 = nc.dram_tensor("woT4", [16, P, D], F32, kind="ExternalInput").ap()
    wupT = nc.dram_tensor("wupT", [D, I3], F32, kind="ExternalInput").ap()
    wgateT = nc.dram_tensor("wgateT", [D, I3], F32, kind="ExternalInput").ap()
    wdownT = nc.dram_tensor("wdownT", [I3, D], F32, kind="ExternalInput").ap()
    wup_n = nc.dram_tensor("wup_n", [I3, D], F32, kind="ExternalInput").ap()
    wgate_n = nc.dram_tensor("wgate_n", [I3, D], F32, kind="ExternalInput").ap()
    wdown_n = nc.dram_tensor("wdown_n", [D, I3], F32, kind="ExternalInput").ap()
    costab = nc.dram_tensor("costab", [P, TA], F32, kind="ExternalInput").ap()
    sintab = nc.dram_tensor("sintab", [P, TA], F32, kind="ExternalInput").ap()
    masks_in = nc.dram_tensor("masks_in", [P, 4 * TCW], BF16, kind="ExternalInput").ap()
    onesc_in = nc.dram_tensor("onesc_in", [P, 8], F32, kind="ExternalInput").ap()
    onesr_in = nc.dram_tensor("onesr_in", [1, P], F32, kind="ExternalInput").ap()
    id_in = nc.dram_tensor("id_in", [P, P], F32, kind="ExternalInput").ap()
    y_out = nc.dram_tensor("y_out", [D, T], F32, kind="ExternalOutput").ap()
    dbg_out = nc.dram_tensor("dbg_out", [P, 96], F32, kind="ExternalOutput").ap()

    u_sc = nc.dram_tensor("u_sc", [I3, T], BF16, kind="Internal").ap()
    g_sc = nc.dram_tensor("g_sc", [I3, T], BF16, kind="Internal").ap()
    qe_sc = nc.dram_tensor("qe_sc", [256, TA], BF16, kind="Internal").ap()
    qo_sc = nc.dram_tensor("qo_sc", [256, TA], BF16, kind="Internal").ap()
    ke_sc = nc.dram_tensor("ke_sc", [256, TA], BF16, kind="Internal").ap()
    ko_sc = nc.dram_tensor("ko_sc", [256, TA], BF16, kind="Internal").ap()
    dqkind = "ExternalOutput" if phases == "dqx" else "Internal"
    wupq_sc = nc.dram_tensor("wupq_sc", [D, I3], BF16, kind=dqkind).ap()
    wgateq_sc = nc.dram_tensor("wgateq_sc", [D, I3], BF16, kind=dqkind).ap()
    wdownq_sc = nc.dram_tensor("wdownq_sc", [I3, D], BF16, kind="Internal").ap()
    rblk_up = nc.dram_tensor("rblk_up", [32, I3], F32, kind=dqkind).ap()
    sblk_up = nc.dram_tensor("sblk_up", [32, I3], F32, kind=dqkind).ap()
    rblk_gt = nc.dram_tensor("rblk_gt", [32, I3], F32, kind="Internal").ap()
    sblk_gt = nc.dram_tensor("sblk_gt", [32, I3], F32, kind="Internal").ap()
    rblk_dn = nc.dram_tensor("rblk_dn", [96, D], F32, kind="Internal").ap()
    sblk_dn = nc.dram_tensor("sblk_dn", [96, D], F32, kind="Internal").ap()
    bar_in = nc.dram_tensor("bar_in", [1, 8], F32, kind="Internal")
    bar_out = nc.dram_tensor("bar_out", [1, 8], F32, kind="Internal")
    scal_sc = nc.dram_tensor("scal_sc", [1, 8], F32, kind="Internal").ap()
    brow_sc = nc.dram_tensor("brow_sc", [1, 64], F32, kind="Internal").ap()
    agx_in = nc.dram_tensor("agx_in", [D, T], BF16, kind="Internal")
    agx_out = nc.dram_tensor("agx_out", [2 * D, T], BF16, kind="Internal")
    aga_in = nc.dram_tensor("aga_in", [512, TA], BF16, kind="Internal")
    aga_out = nc.dram_tensor("aga_out", [2 * 512, TA], BF16, kind="Internal")

    SB0 = (nc.sbuf_base + 31) // 32 * 32

    def alloc(name, shape, dt, at):
        return nc.alloc_sbuf_tensor_at(name, shape, dt, offset=SB0 + at)

    x8 = [alloc(f"x{i}", [P, T], F32, i * 8192) for i in range(FC)]
    RES = 64 * 1024
    scales = alloc("scales", [P, 64], F32, RES)
    onesc = alloc("onesc", [P, 8], F32, RES + 256)
    onescb = alloc("onescb", [P, 8], BF16, RES + 288)
    onesr = alloc("onesr", [1, P], F32, RES + 320)
    onesrb = alloc("onesrb", [1, P], BF16, RES + 832)
    row32 = alloc("row32", [1, 32], F32, RES + 1088)
    rowtmp = alloc("rowtmp", [1, P], F32, RES + 1216)
    masks_t = alloc("masks_t", [P, 4 * TCW], BF16, RES + 1728)
    mu_row = alloc("mu_row", [1, T], F32, RES + 5824)
    r_row = alloc("r_row", [1, T], F32, RES + 14016)
    mx = alloc("mx", [P, 200], F32, RES + 22208)
    tmpa = alloc("tmpa", [P, TCW], F32, RES + 23008)
    tmpb = alloc("tmpb", [P, TCW], F32, RES + 25056)
    xb2 = [alloc(f"xb2_{i}", [P, TCW], BF16, RES + 27104 + i * 1024) for i in range(2)]
    x2b = alloc("x2b", [P, TCW], BF16, RES + 29152)
    id128 = alloc("id128", [P, P], F32, RES + 30176)

    WORK = 94 * 1024 + 256
    BIG = WORK + 56 * 1024
    assert BIG + 56 * 1024 <= 212831

    xn8 = [alloc(f"xn{i}", [P, T], BF16, WORK + i * 4096) for i in range(FC)]
    wn_raw = alloc("wn_raw", [P, I3], F32, WORK + 0)
    sn_t = alloc("sn_t", [P, 96], F32, WORK + 12 * 1024)
    sn_r = alloc("sn_r", [P, 96], F32, WORK + 12 * 1024 + 512)
    sn_s = alloc("sn_s", [P, 96], F32, WORK + 12 * 1024 + 1024)
    sn_ev = alloc("sn_ev", [P, 160], F32, WORK + 12 * 1024 + 1536)
    dq_raw = [alloc(f"dq_raw{i}", [P, I3], F32, WORK + 14 * 1024 + i * 12 * 1024) for i in range(2)]
    p2_stg = [alloc(f"p2_stg{i}", [P, T], BF16, WORK + 38 * 1024 + i * 4096) for i in range(2)]
    p2_w = [alloc(f"p2_w{i}", [P, FC * P], BF16, WORK + 46 * 1024 + i * 2048) for i in range(2)]

    xq8 = [alloc(f"xq{i}", [P, T], BF16, BIG + i * 4096) for i in range(FC)]
    g2b = [alloc(f"g2b{i}", [P, T], BF16, WORK + i * 4096) for i in range(14)] + \
          [alloc(f"g2b{i}", [P, T], BF16, BIG + (i - 14) * 4096) for i in range(14, IC)]
    TH = T // 2
    tmpA2 = alloc("tmpA2", [P, TH], F32, BIG + 40 * 1024)
    tmpB2 = alloc("tmpB2", [P, TH], F32, BIG + 44 * 1024)
    tmpf1 = alloc("tmpf1", [P, TH], F32, BIG + 48 * 1024)
    strm = [alloc(f"strm{i}", [P, TH], BF16, BIG + 52 * 1024 + i * 2048) for i in range(2)]
    xqi = alloc("xqi", [P, T], F32, BIG + 32 * 1024)
    DQW = 1536
    dq_e1 = alloc("dq_e1", [P, DQW], F32, BIG + 32 * 1024)
    dq_e2 = alloc("dq_e2", [P, DQW], F32, BIG + 38 * 1024)
    dq_t = alloc("dq_t", [P, DQW], F32, BIG + 44 * 1024)
    dq_out = [alloc(f"dq_out{i}", [P, DQW], BF16, BIG + 50 * 1024 + i * 3072) for i in range(2)]
    p5_w = alloc("p5_w", [P, IC * P], BF16, BIG + 46 * 1024)

    xs_big = [alloc(f"xs_big{i}", [P, TA], BF16, WORK + i * 8192) for i in range(2)]
    cos_t = alloc("cos_t", [P, TA], F32, WORK + 16 * 1024)
    sin_t = alloc("sin_t", [P, TA], F32, WORK + 32 * 1024)
    rope_t1 = alloc("rope_t1", [P, TCW], F32, WORK + 48 * 1024)
    rope_t2 = alloc("rope_t2", [P, TCW], F32, WORK + 50 * 1024)
    eo_stg = [alloc(f"eo_stg{i}", [P, TCW], BF16, WORK + 52 * 1024 + i * 1024) for i in range(2)]
    qk_all = [alloc(f"qk_all{i}", [P, TA], BF16, WORK + i * 8192) for i in range(7)] + \
             [alloc("qk_all7", [P, TA], BF16, BIG + 45 * 1024)]
    wo16 = [alloc(f"wo16_{i}", [P, D], BF16, WORK + i * 2048) for i in range(16)]

    wqk_b = [alloc(f"wqk_b{i}", [P, 1024], BF16, BIG + i * 2048) for i in range(FC)]
    wv_b = [alloc(f"wv_b{i}", [P, 512], BF16, WORK + 16 * 1024 + i * 1024) for i in range(FC)]
    wqkv_stage = alloc("wqkv_stage", [P, 1536], F32, BIG + 16 * 1024)
    v_all = [alloc(f"v_all{i}", [P, NH * 65], BF16, BIG + i * 1056) for i in range(32)]
    ao_stg = alloc("ao_stg", [P, TA], BF16, BIG + 33 * 1024)
    expS = [alloc(f"expS{i}", [P, TCW], BF16, BIG + 41 * 1024 + i * 1024) for i in range(2)]
    mrep = alloc("mrep", [P, 80], F32, BIG + 43 * 1024)
    nq_row = alloc("nq_row", [1, 128], F32, BIG + 43 * 1024 + 320)
    recip_row = alloc("recip_row", [1, TCW], BF16, BIG + 44 * 1024)
    agaS = [alloc(f"agaS{i}", [P, TCW], BF16, BIG + i * 1024) for i in range(16)]
    wo_raw = [alloc(f"wo_raw{i}", [P, D], F32, BIG + 16 * 1024 + i * 4096) for i in range(2)]

    pb = [nc.alloc_psum_tensor(f"pb{i}", [P, TCW], F32) for i in range(8)]
    nc.const_aps.register(F32, MAGIC, onesc[:, 2:3])
    nc.const_aps.register(F32, 1e-5, onesc[:, 1:2])

    prog = Prog(nc)
    sQ = Sem(nc, "sQ")
    sG = Sem(nc, "sG")
    sT = Sem(nc, "sT")
    sV = Sem(nc, "sV")
    sA = Sem(nc, "sA")
    sC = Sem(nc, "sC")
    sO = Sem(nc, "sO")

    def mm(psum_ap, lhsT, rhs, start, stop):
        prog.emit("tensor", lambda t, a=psum_ap, b=lhsT, c=rhs, s1=start, s2=stop:
                  t.matmul(a, b, c, start=s1, stop=s2))

    def t_mark():
        last = prog.q["tensor"].pop()
        h = sT.h
        prog.q["tensor"].append(lambda t, last=last: last(t).then_inc(h, 1))
        sT.n += 1
        return (sT, sT.n)

    def v_op(fn):
        return prog.inst("vector", sV, fn)

    def v_chain(fn):
        W("vector", (sV, sV.n))
        return prog.inst("vector", sV, fn)

    def s_op(fn):
        return prog.inst("scalar", sA, fn)

    W = prog.wait

    def sync_all():
        snaps = [(sT, sT.n), (sV, sV.n), (sA, sA.n), (sQ, sQ.n), (sG, sG.n)]
        for e in ("sync", "tensor", "vector", "scalar", "gpsimd"):
            for sn in snaps:
                W(e, sn)

    for i in range(FC):
        prog.dma("sync", sQ, x8[i][:, :], x_in[i * P:(i + 1) * P, :])
    prog.dma("sync", sQ, masks_t[:, :], masks_in[:, :])
    prog.dma("sync", sQ, onesc[:, :], onesc_in[:, :])
    prog.dma("sync", sQ, onesr[:, :], onesr_in[:, :])
    prog.dma("sync", sQ, id128[:, :], id_in[:, :])
    loads = (sQ, sQ.n)
    W("vector", loads)
    v_op(lambda v: v.tensor_copy(onescb[:, :], onesc[:, :]))
    v_op(lambda v: v.tensor_copy(onesrb[:, :], onesr[:, :]))
    v_op(lambda v: v.memset(row32[:, :], 0.0))
    v_op(lambda v: v.memset(rowtmp[:, :], 0.0))
    sync_all()

    def barrier_max():
        vs = (sV, sV.n)
        W("gpsimd", vs)
        w = prog.dma("gpsimd", sG, bar_in.ap()[0:1, 0:8], row32[0:1, 0:8])
        W("gpsimd", w)
        c = prog.inst("gpsimd", sC, lambda g: g.collective_compute(
            "AllReduce", ALU.max, replica_groups=[list(range(8))],
            ins=[bar_in.ap()], outs=[bar_out.ap()]))
        W("gpsimd", c)
        r = prog.dma("gpsimd", sG, row32[0:1, 8:16], bar_out.ap()[0:1, 0:8])
        W("vector", r)
        return r

    def replicate(src_col, dst_col, n):
        vs = (sV, sV.n)
        W("gpsimd", vs)
        w = prog.dma("gpsimd", sG, scal_sc[0:1, 0:n], row32[0:1, src_col:src_col + n])
        W("gpsimd", w)
        r = prog.dma("gpsimd", sG, scales[:, dst_col:dst_col + n],
                     scal_sc[0:1, 0:n].broadcast_to([P, n]))
        return r

    def colmax(lo, n, col):
        v_chain(lambda v: v.tensor_reduce(mx[:, 196:197], mx[:, lo:lo + n], AX.X,
                                          ALU.max, apply_absolute_value=True))
        vd = (sV, sV.n)
        W("gpsimd", vd)
        d = prog.dma("gpsimd", sG, rowtmp[0:1, 0:P], mx[:, 196:197])
        W("vector", d)
        v_chain(lambda v: v.tensor_reduce(row32[0:1, col:col + 1], rowtmp[0:1, 0:P],
                                          AX.X, ALU.max, apply_absolute_value=True))

    def emit_ln(dst_fn, do_max, nt=NT):
        last_rep = [None]
        for t in range(nt):
            cs = slice(t * TCW, (t + 1) * TCW)
            for i in range(FC):
                b = i & 1
                c1 = v_op(lambda v, i=i, cs=cs, b=b: v.tensor_copy(xb2[b][:, :], x8[i][:, cs]))
                c2 = s_op(lambda s, i=i, cs=cs: s.activation(x2b[:, :], x8[i][:, cs], AF.Square))
                W("tensor", c1)
                W("tensor", c2)
                mm(pb[0][0:1, :], onescb[:, 0:1], xb2[b][:, :], start=(i == 0), stop=(i == FC - 1))
                mm(pb[1][0:1, :], onescb[:, 0:1], x2b[:, :], start=(i == 0), stop=(i == FC - 1))
                ms = t_mark()
                if i < FC - 1:
                    W("vector", ms)
                    W("scalar", ms)
            tdone = (sT, sT.n)
            W("vector", tdone)
            v_op(lambda v, cs=cs: v.tensor_scalar(mu_row[0:1, cs], pb[0][0:1, :], 1.0 / D, None, ALU.mult))
            v_op(lambda v, cs=cs: v.tensor_scalar(r_row[0:1, cs], pb[1][0:1, :], 1.0 / D, None, ALU.mult))
            v_chain(lambda v, cs=cs: v.tensor_tensor(tmpa[0:1, :], mu_row[0:1, cs], mu_row[0:1, cs], ALU.mult))
            vv = v_chain(lambda v, cs=cs: v.tensor_tensor(r_row[0:1, cs], r_row[0:1, cs], tmpa[0:1, :], ALU.subtract))
            W("scalar", vv)
            sq = s_op(lambda s, cs=cs: s.activation(r_row[0:1, cs], r_row[0:1, cs], AF.Sqrt, bias=1e-5))
            W("vector", sq)
            rd = v_op(lambda v, cs=cs: v.reciprocal(r_row[0:1, cs], r_row[0:1, cs]))
            W("tensor", rd)
            if last_rep[0] is not None:
                W("tensor", last_rep[0])
            mm(pb[2][:, :], onesr[0:1, 0:P], mu_row[0:1, cs], start=True, stop=True)
            mm(pb[3][:, :], onesr[0:1, 0:P], r_row[0:1, cs], start=True, stop=True)
            rep = t_mark()
            W("vector", rep)
            for i in range(FC):
                v_op(lambda v, i=i, cs=cs: v.tensor_tensor(tmpa[:, :], x8[i][:, cs], pb[2][:, :], ALU.subtract))
                v_op(lambda v, i=i, cs=cs: v.tensor_tensor(tmpb[:, :], tmpa[:, :], pb[3][:, :], ALU.mult))
                v_op(lambda v, i=i, t=t: v.tensor_copy(dst_fn(i, t), tmpb[:, :]))
                if do_max:
                    v_op(lambda v, i=i, t=t: v.tensor_reduce(
                        mx[:, (t * FC + i):(t * FC + i + 1)], tmpb[:, :], AX.X,
                        ALU.max, apply_absolute_value=True))
            last_rep[0] = (sV, sV.n)

    def emit_scaleprep(nat_ap, nchunks, nblk, rblk_ap, sblk_ap, width):
        W("sync", (sV, sV.n))
        prev_ld = [None]
        for c in range(nchunks):
            if prev_ld[0] is not None:
                W("sync", prev_ld[0])
            ld = prog.dma("sync", sQ, wn_raw[:, 0:width], nat_ap[c * P:(c + 1) * P, :])
            W("vector", ld)
            v_op(lambda v, nblk=nblk, width=width: v.tensor_reduce(
                sn_t[:, 0:nblk], wn_raw[:, 0:width].rearrange("p (b k) -> p b k", k=32),
                AX.X, ALU.max, apply_absolute_value=True))
            v_chain(lambda v, nblk=nblk: v.tensor_scalar(sn_t[:, 0:nblk], sn_t[:, 0:nblk],
                                                         7e-12, None, ALU.max))
            v_chain(lambda v, nblk=nblk: v.reciprocal(sn_r[:, 0:nblk], sn_t[:, 0:nblk]))
            v_chain(lambda v, nblk=nblk: v.tensor_scalar(sn_r[:, 0:nblk], sn_r[:, 0:nblk],
                                                         7.0, None, ALU.mult))
            vd = v_chain(lambda v, nblk=nblk: v.tensor_scalar(sn_s[:, 0:nblk], sn_t[:, 0:nblk],
                                                              1.0 / 7.0, None, ALU.mult))
            W("tensor", vd)
            tr1 = prog.inst("tensor", sT, lambda t, nblk=nblk: t.transpose(
                pb[4][0:nblk, 0:P], sn_r[:, 0:nblk], id128[:, :]))
            tr2 = prog.inst("tensor", sT, lambda t, nblk=nblk: t.transpose(
                pb[5][0:nblk, 0:P], sn_s[:, 0:nblk], id128[:, :]))
            W("vector", tr2)
            e1 = v_op(lambda v, nblk=nblk: v.tensor_copy(sn_ev[0:nblk, 0:P], pb[4][0:nblk, 0:P]))
            W("gpsimd", e1)
            d1 = prog.dma("gpsimd", sG, rblk_ap[0:nblk, c * P:(c + 1) * P], sn_ev[0:nblk, 0:P])
            W("vector", d1)
            e2 = v_op(lambda v, nblk=nblk: v.tensor_copy(sn_ev[0:nblk, 0:P], pb[5][0:nblk, 0:P]))
            W("tensor", e2)
            W("gpsimd", e2)
            d2 = prog.dma("gpsimd", sG, sblk_ap[0:nblk, c * P:(c + 1) * P], sn_ev[0:nblk, 0:P])
            W("vector", d2)
            prev_ld[0] = (sV, sV.n)
        return (sG, sG.n)

    def emit_dequant(wT_ap, rblk_ap, sblk_ap, cache_ap, nfc, owidth):
        prev = {}
        nstrip = max(1, owidth // DQW)
        sw = owidth // nstrip
        assert sw <= DQW
        for f in range(nfc):
            for hh in range(nstrip):
                cs = slice(hh * sw, (hh + 1) * sw)
                ob = (f * nstrip + hh) & 1
                ld = prog.dma("sync", sQ, dq_raw[f & 1][:, 0:sw], wT_ap[f * P:(f + 1) * P, cs])
                l1 = prog.dma("sync", sQ, dq_e1[:, 0:sw],
                              rblk_ap[4 * f:4 * f + 4, cs].unsqueeze(1).broadcast_to([4, 32, sw]))
                l2 = prog.dma("sync", sQ, dq_e2[:, 0:sw],
                              sblk_ap[4 * f:4 * f + 4, cs].unsqueeze(1).broadcast_to([4, 32, sw]))
                W("vector", ld)
                W("vector", l1)
                if ob in prev:
                    W("vector", prev[ob])
                m1 = v_op(lambda v, f=f, sw=sw: v.tensor_tensor(dq_e1[:, 0:sw], dq_raw[f & 1][:, 0:sw],
                                                         dq_e1[:, 0:sw], ALU.mult))
                W("scalar", m1)
                r1 = s_op(lambda s, sw=sw: s.activation(dq_t[:, 0:sw], dq_e1[:, 0:sw], AF.Copy,
                                                        bias=MAGIC))
                W("vector", r1)
                W("vector", l2)
                m2 = v_op(lambda v, ob=ob, sw=sw: v.scalar_tensor_tensor(
                    dq_out[ob][:, 0:sw], dq_t[:, 0:sw], -MAGIC, dq_e2[:, 0:sw], ALU.add, ALU.mult))
                W("gpsimd", m2)
                cd = prog.dma("gpsimd", sG, cache_ap[f * P:(f + 1) * P, cs], dq_out[ob][:, 0:sw])
                prev[ob] = cd
                W("sync", r1)
                W("sync", m2)
        return (sG, sG.n)

    def emit_p2(n_mat=2, n_oc=IC):
        prev_stg = {}
        prev_bank = {}
        for mi, (cache_ap, out_ap) in enumerate(((wupq_sc, u_sc), (wgateq_sc, g_sc))[:n_mat]):
            for oc in range(n_oc):
                wb = oc & 1
                ld = prog.dma("sync", sQ, p2_w[wb][:, :].rearrange("p (f c) -> p f c", c=P),
                              cache_ap.rearrange("(f p) o -> p f o", p=P)[:, :, oc * P:(oc + 1) * P])
                W("tensor", ld)
                sg = oc & 1
                if sg in prev_stg:
                    W("scalar", prev_stg[sg])
                import os
                dbg = os.environ.get("P2DBG", "")
                for tc in range(NT):
                    bank = pb[tc & 1]
                    if (tc & 1) in prev_bank:
                        for sn in prev_bank[tc & 1]:
                            W("tensor", sn)
                    for f in range(FC):
                        mm(bank[:, :], p2_w[wb][:, f * P:(f + 1) * P],
                           xq8[f][:, tc * TCW:(tc + 1) * TCW],
                           start=(f == 0), stop=(f == FC - 1))
                    md = t_mark()
                    W("scalar", md)
                    if dbg == "vevac":
                        ev = v_op(lambda v, sg=sg, tc=tc, bank=bank: v.tensor_copy(
                            p2_stg[sg][:, tc * TCW:(tc + 1) * TCW], bank[:, :]))
                    elif dbg == "noevac":
                        ev = v_op(lambda v, sg=sg, tc=tc: v.memset(
                            p2_stg[sg][:, tc * TCW:(tc + 1) * TCW], 0.0))
                    else:
                        ev = s_op(lambda s, sg=sg, tc=tc, bank=bank: s.activation(
                            p2_stg[sg][:, tc * TCW:(tc + 1) * TCW], bank[:, :], AF.Copy))
                    W("vector", md)
                    W("vector", ev)
                    mxc = mi * 96 + oc * 4 + tc
                    if dbg in ("nored", "noevac"):
                        vr = v_op(lambda v, mxc=mxc: v.memset(mx[:, mxc:mxc + 1], 0.0))
                    else:
                        vr = v_op(lambda v, bank=bank, mxc=mxc: v.tensor_reduce(
                            mx[:, mxc:mxc + 1], bank[:, :], AX.X, ALU.max, apply_absolute_value=True))
                    prev_bank[tc & 1] = (ev, vr)
                evd = (sA, sA.n)
                W("gpsimd", evd)
                st = prog.dma("gpsimd", sG, out_ap[oc * P:(oc + 1) * P, :], p2_stg[sg][:, :])
                prev_stg[sg] = st
                W("sync", (sT, sT.n))

    def emit_p3():
        spill_done = (sG, sG.n)
        W("sync", spill_done)
        W("sync", (sA, sA.n))
        W("sync", (sV, sV.n))
        prev = {}
        for ic in range(IC):
            for hh in range(2):
                b = (2 * ic + hh) & 1
                ts_ = slice(hh * TH, (hh + 1) * TH)
                if b in prev:
                    W("sync", prev[b])
                ld = prog.dma("sync", sQ, strm[b][:, :], g_sc[ic * P:(ic + 1) * P, ts_])
                W("scalar", ld)
                a1 = s_op(lambda s, b=b: s.activation(tmpA2[:, :], strm[b][:, :], AF.Copy,
                                                      scale=scales[:, 5:6], bias=MAGIC))
                a2 = s_op(lambda s: s.activation(tmpf1[:, :], tmpA2[:, :], AF.Sigmoid,
                                                 scale=scales[:, 4:5], bias=scales[:, 6:7]))
                a3 = s_op(lambda s: s.activation(tmpB2[:, :], tmpf1[:, :], AF.Copy,
                                                 scale=65536.0, bias=MAGIC))
                W("vector", a3)
                sc_ = v_op(lambda v: v.tensor_scalar(tmpB2[:, :], tmpB2[:, :], -MAGIC, 65535.0,
                                                     ALU.add, ALU.min))
                g2 = v_op(lambda v, ic=ic, ts_=ts_: v.scalar_tensor_tensor(
                    g2b[ic][:, ts_], tmpA2[:, :], -MAGIC, tmpB2[:, :], ALU.add, ALU.mult))
                vr = v_op(lambda v, ic=ic, hh=hh, ts_=ts_: v.tensor_reduce(
                    mx[:, 2 * ic + hh:2 * ic + hh + 1], g2b[ic][:, ts_],
                    AX.X, ALU.max, apply_absolute_value=True))
                W("scalar", g2)
                prev[b] = vr

    def emit_p4():
        W("sync", (sA, sA.n))
        W("sync", (sV, sV.n))
        prev = {}
        for ic in range(IC):
            for hh in range(2):
                b = (2 * ic + hh) & 1
                ts_ = slice(hh * TH, (hh + 1) * TH)
                if b in prev:
                    W("sync", prev[b])
                ld = prog.dma("sync", sQ, strm[b][:, :], u_sc[ic * P:(ic + 1) * P, ts_])
                W("scalar", ld)
                a1 = s_op(lambda s, b=b: s.activation(tmpA2[:, :], strm[b][:, :], AF.Copy,
                                                      scale=scales[:, 3:4], bias=MAGIC))
                a2 = s_op(lambda s, ic=ic, ts_=ts_: s.activation(tmpB2[:, :], g2b[ic][:, ts_], AF.Copy,
                                                                scale=scales[:, 9:10], bias=MAGIC))
                W("vector", a2)
                g2q = v_op(lambda v: v.tensor_scalar(tmpB2[:, :], tmpB2[:, :], -MAGIC, None, ALU.add))
                ov = v_op(lambda v, ic=ic, ts_=ts_: v.scalar_tensor_tensor(
                    g2b[ic][:, ts_], tmpA2[:, :], -MAGIC, tmpB2[:, :], ALU.add, ALU.mult))
                vr = v_op(lambda v, ic=ic, hh=hh, ts_=ts_: v.tensor_reduce(
                    mx[:, 2 * ic + hh:2 * ic + hh + 1], g2b[ic][:, ts_],
                    AX.X, ALU.max, apply_absolute_value=True))
                W("scalar", ov)
                prev[b] = vr

    def emit_p5():
        for ic in range(IC):
            for hh in range(2):
                ts_ = slice(hh * TH, (hh + 1) * TH)
                a1 = s_op(lambda s, ic=ic, ts_=ts_: s.activation(tmpA2[:, :], g2b[ic][:, ts_], AF.Copy,
                                                                scale=scales[:, 10:11], bias=MAGIC))
                W("vector", a1)
                cv = v_op(lambda v, ic=ic, ts_=ts_: v.tensor_scalar(
                    g2b[ic][:, ts_], tmpA2[:, :], -MAGIC, None, ALU.add))
                W("scalar", cv)
        oq_done = (sV, sV.n)
        W("tensor", oq_done)
        W("sync", oq_done)
        prev_w = None
        prev_bank = {}
        for fo in range(FC):
            if prev_w is not None:
                W("sync", prev_w)
            ld = prog.dma("sync", sQ, p5_w[:, :].rearrange("p (f c) -> p f c", c=P),
                          wdownq_sc.rearrange("(f p) o -> p f o", p=P)[:, :, fo * P:(fo + 1) * P])
            W("tensor", ld)
            for tc in range(NT):
                bank = pb[4 + (tc & 1)]
                if (tc & 1) in prev_bank:
                    W("tensor", prev_bank[tc & 1])
                for ic in range(IC):
                    mm(bank[:, :], p5_w[:, ic * P:(ic + 1) * P],
                       g2b[ic][:, tc * TCW:(tc + 1) * TCW],
                       start=(ic == 0), stop=(ic == IC - 1))
                md = t_mark()
                W("vector", md)
                res = v_op(lambda v, fo=fo, tc=tc, bank=bank: v.scalar_tensor_tensor(
                    x8[fo][:, tc * TCW:(tc + 1) * TCW], bank[:, :], scales[:, 11:12],
                    x8[fo][:, tc * TCW:(tc + 1) * TCW], ALU.mult, ALU.add))
                prev_bank[tc & 1] = res
            prev_w = (sT, sT.n)

    def rv(fn):
        return v_chain(fn)

    def derive_b1():
        rv(lambda v: v.tensor_scalar(row32[0:1, 16:17], row32[0:1, 8:9], FLOOR, None, ALU.max))
        rv(lambda v: v.reciprocal(row32[0:1, 17:18], row32[0:1, 16:17]))
        rv(lambda v: v.tensor_scalar(row32[0:1, 24:25], row32[0:1, 17:18], QMAX, None, ALU.mult))
        rv(lambda v: v.tensor_scalar(row32[0:1, 25:26], row32[0:1, 16:17], 1.0 / QMAX, None, ALU.mult))
        rv(lambda v: v.tensor_copy(row32[0:1, 18:19], row32[0:1, 25:26]))
        return replicate(24, 0, 2)

    def derive_b2():
        for (src, sc_c, rc_c, keep) in ((8, 24, 25, 19), (9, 26, 27, 20)):
            rv(lambda v, src=src: v.tensor_tensor(row32[0:1, 16:17], row32[0:1, src:src + 1],
                                                  row32[0:1, 18:19], ALU.mult))
            rv(lambda v: v.tensor_scalar(row32[0:1, 16:17], row32[0:1, 16:17], FLOOR, None, ALU.max))
            rv(lambda v, sc=sc_c: v.tensor_scalar(row32[0:1, sc:sc + 1], row32[0:1, 16:17],
                                                  1.0 / QMAX, None, ALU.mult))
            rv(lambda v: v.reciprocal(row32[0:1, 17:18], row32[0:1, 16:17]))
            rv(lambda v: v.tensor_scalar(row32[0:1, 17:18], row32[0:1, 17:18], QMAX, None, ALU.mult))
            rv(lambda v, rc=rc_c: v.tensor_tensor(row32[0:1, rc:rc + 1], row32[0:1, 17:18],
                                                  row32[0:1, 18:19], ALU.mult))
            rv(lambda v, sc=sc_c, keep=keep: v.tensor_copy(row32[0:1, keep:keep + 1],
                                                           row32[0:1, sc:sc + 1]))
        rv(lambda v: v.tensor_scalar(row32[0:1, 28:29], row32[0:1, 26:27], -MAGIC, None, ALU.mult))
        return replicate(24, 2, 5)

    def derive_b3():
        rv(lambda v: v.tensor_scalar(row32[0:1, 21:22], row32[0:1, 20:21], 1.0 / 65536.0, None, ALU.mult))
        rv(lambda v: v.tensor_tensor(row32[0:1, 16:17], row32[0:1, 8:9], row32[0:1, 21:22], ALU.mult))
        rv(lambda v: v.tensor_scalar(row32[0:1, 16:17], row32[0:1, 16:17], FLOOR, None, ALU.max))
        rv(lambda v: v.tensor_scalar(row32[0:1, 24:25], row32[0:1, 16:17], 1.0 / QMAX, None, ALU.mult))
        rv(lambda v: v.reciprocal(row32[0:1, 17:18], row32[0:1, 16:17]))
        rv(lambda v: v.tensor_scalar(row32[0:1, 17:18], row32[0:1, 17:18], QMAX, None, ALU.mult))
        rv(lambda v: v.tensor_tensor(row32[0:1, 25:26], row32[0:1, 17:18], row32[0:1, 21:22], ALU.mult))
        rv(lambda v: v.tensor_tensor(row32[0:1, 22:23], row32[0:1, 24:25], row32[0:1, 19:20], ALU.mult))
        return replicate(24, 8, 2)

    def derive_b4():
        rv(lambda v: v.tensor_tensor(row32[0:1, 16:17], row32[0:1, 8:9], row32[0:1, 22:23], ALU.mult))
        rv(lambda v: v.tensor_scalar(row32[0:1, 16:17], row32[0:1, 16:17], FLOOR, None, ALU.max))
        rv(lambda v: v.tensor_scalar(row32[0:1, 23:24], row32[0:1, 16:17], 1.0 / QMAX, None, ALU.mult))
        rv(lambda v: v.reciprocal(row32[0:1, 17:18], row32[0:1, 16:17]))
        rv(lambda v: v.tensor_scalar(row32[0:1, 17:18], row32[0:1, 17:18], QMAX, None, ALU.mult))
        rv(lambda v: v.tensor_tensor(row32[0:1, 24:25], row32[0:1, 17:18], row32[0:1, 22:23], ALU.mult))
        rv(lambda v: v.tensor_scalar(row32[0:1, 25:26], row32[0:1, 23:24], 0.5, None, ALU.mult))
        return replicate(24, 10, 2)

    def emit_ffn(call_idx):
        emit_ln(lambda i, t: xn8[i][:, t * TCW:(t + 1) * TCW], do_max=True,
                nt=(1 if phases == "ln1" else NT))
        if phases in ("lnx", "ln1"):
            sync_all()
            return
        colmax(0, NT * FC, 0)
        if phases == "cm":
            sync_all()
            return
        barrier_max()
        rep1 = derive_b1()
        W("scalar", rep1)
        if phases == "bar":
            sync_all()
            return
        for i in range(FC):
            a = s_op(lambda s, i=i: s.activation(xqi[:, :], xn8[i][:, :], AF.Copy,
                                                 scale=scales[:, 0:1], bias=MAGIC))
            W("vector", a)
            cv = v_op(lambda v, i=i: v.tensor_scalar(xq8[i][:, :], xqi[:, :], -MAGIC, None, ALU.add))
            W("scalar", cv)
        if phases == "ln":
            sync_all()
            return
        if call_idx == 0:
            sync_all()
            emit_scaleprep(wup_n, IC, 32, rblk_up, sblk_up, D)
            emit_scaleprep(wgate_n, IC, 32, rblk_gt, sblk_gt, D)
            emit_scaleprep(wdown_n, FC, 96, rblk_dn, sblk_dn, I3)
            sp_done = (sG, sG.n)
            W("sync", sp_done)
            emit_dequant(wupT, rblk_up, sblk_up, wupq_sc, FC, I3)
            emit_dequant(wgateT, rblk_gt, sblk_gt, wgateq_sc, FC, I3)
            emit_dequant(wdownT, rblk_dn, sblk_dn, wdownq_sc, IC, D)
            dq_done = (sG, sG.n)
            W("sync", dq_done)
        sync_all()
        if phases in ("dq", "dqx"):
            return
        if phases == "p2a":
            emit_p2(1, 1)
            sync_all()
            return
        if phases == "p2b":
            emit_p2(1, IC)
            sync_all()
            return
        emit_p2()
        colmax(0, 96, 0)
        colmax(96, 96, 1)
        if phases in ("p2", "p2x", "p2y"):
            sync_all()
            return
        barrier_max()
        rep2 = derive_b2()
        W("scalar", rep2)
        emit_p3()
        if phases == "p3":
            sync_all()
            return
        colmax(0, 2 * IC, 0)
        barrier_max()
        rep3 = derive_b3()
        W("scalar", rep3)
        emit_p4()
        if phases == "p4":
            sync_all()
            return
        colmax(0, 2 * IC, 0)
        barrier_max()
        rep4 = derive_b4()
        W("scalar", rep4)
        emit_p5()
        sync_all()

    def emit_attention():
        emit_ln(lambda i, t: xn8[i][:, t * TCW:(t + 1) * TCW], do_max=False)
        ln2 = (sV, sV.n)
        W("gpsimd", ln2)
        for i in range(FC):
            prog.dma("gpsimd", sG, agx_in.ap()[i * P:(i + 1) * P, :], xn8[i][:, :])
        agw = (sG, sG.n)
        W("gpsimd", agw)
        agc = prog.inst("gpsimd", sC, lambda g: g.collective_compute(
            "AllGather", ALU.bypass, replica_groups=[[0, 1], [2, 3], [4, 5], [6, 7]],
            ins=[agx_in.ap()], outs=[agx_out.ap()]))
        W("sync", agw)
        prog.dma("sync", sQ, cos_t[:, :], costab[:, :])
        prog.dma("sync", sQ, sin_t[:, :], sintab[:, :])
        prev_cv = [None]
        for i in range(FC):
            if prev_cv[0] is not None:
                W("sync", prev_cv[0])
            ld = prog.dma("sync", sQ, wqkv_stage[:, :], wqkv_stage[:, :]) if False else \
                 prog.dma("sync", sQ, wqkv_stage[:, 0:1024], wqkvT[i * P:(i + 1) * P, 0:1024])
            W("vector", ld)
            cv = v_op(lambda v, i=i: v.tensor_copy(wqk_b[i][:, :], wqkv_stage[:, 0:1024]))
            prev_cv[0] = cv
        W("sync", agc)
        tabs = (sQ, sQ.n)
        W("tensor", tabs)
        W("vector", tabs)
        W("tensor", (sV, sV.n))

        rope_pairs = [(0, 2), (1, 3), (4, 6), (5, 7)]
        prev_eo = [None]
        prev_xs_use = {}
        for tc in range(NTA):
            contrib = 0 if tc < NT else 1
            ccols = slice((tc % NT) * TCW, (tc % NT + 1) * TCW)
            b = tc & 1
            if b in prev_xs_use:
                W("sync", prev_xs_use[b])
            ld = prog.dma("sync", sQ,
                          xs_big[b][:, :].rearrange("p (f c) -> p f c", c=TCW),
                          agx_out.ap()[contrib * D:(contrib + 1) * D, ccols]
                          .rearrange("(f p) c -> p f c", p=P))
            W("tensor", ld)
            acols = slice(tc * TCW, (tc + 1) * TCW)
            for pi, (eoc, ooc) in enumerate(rope_pairs):
                pE, pO = pb[(pi & 1) * 2], pb[(pi & 1) * 2 + 1]
                for f in range(FC):
                    mm(pE[:, :], wqk_b[f][:, eoc * P:(eoc + 1) * P],
                       xs_big[b][:, f * TCW:(f + 1) * TCW], start=(f == 0), stop=(f == FC - 1))
                for f in range(FC):
                    mm(pO[:, :], wqk_b[f][:, ooc * P:(ooc + 1) * P],
                       xs_big[b][:, f * TCW:(f + 1) * TCW], start=(f == 0), stop=(f == FC - 1))
                md = t_mark()
                W("vector", md)
                if prev_eo[0] is not None:
                    W("vector", prev_eo[0])
                v_op(lambda v, pE=pE, acols=acols: v.tensor_tensor(rope_t1[:, :], pE[:, :], cos_t[:, acols], ALU.mult))
                v_op(lambda v, pO=pO, acols=acols: v.tensor_tensor(rope_t2[:, :], pO[:, :], sin_t[:, acols], ALU.mult))
                v_op(lambda v: v.tensor_tensor(eo_stg[0][:, :], rope_t1[:, :], rope_t2[:, :], ALU.subtract))
                v_op(lambda v, pE=pE, acols=acols: v.tensor_tensor(rope_t1[:, :], pE[:, :], sin_t[:, acols], ALU.mult))
                v_op(lambda v, pO=pO, acols=acols: v.tensor_tensor(rope_t2[:, :], pO[:, :], cos_t[:, acols], ALU.mult))
                ve = v_op(lambda v: v.tensor_tensor(eo_stg[1][:, :], rope_t1[:, :], rope_t2[:, :], ALU.add))
                W("tensor", ve)
                W("gpsimd", ve)
                edst = qe_sc if pi < 2 else ke_sc
                odst = qo_sc if pi < 2 else ko_sc
                rr = 128 * (pi & 1)
                prog.dma("gpsimd", sG, edst[rr:rr + P, acols], eo_stg[0][:, :])
                d2 = prog.dma("gpsimd", sG, odst[rr:rr + P, acols], eo_stg[1][:, :])
                prev_eo[0] = d2
            prev_xs_use[b] = (sT, sT.n)
        rope_done_t = (sT, sT.n)
        rope_done_v = (sV, sV.n)
        W("sync", rope_done_t)
        W("sync", rope_done_v)
        prev_wv = [None]
        for i in range(FC):
            if prev_wv[0] is not None:
                W("sync", prev_wv[0])
            ldv = prog.dma("sync", sQ, tmpa[:, :], wqkvT[i * P:(i + 1) * P, 1024:1536])
            W("vector", ldv)
            cvv = v_op(lambda v, i=i: v.tensor_copy(wv_b[i][:, :], tmpa[:, :]))
            prev_wv[0] = cvv
        wv_done = (sV, sV.n)
        W("tensor", wv_done)
        prev_vb = {}
        for tc in range(NTA):
            contrib = 0 if tc < NT else 1
            ccols = slice((tc % NT) * TCW, (tc % NT + 1) * TCW)
            b = tc & 1
            if b in prev_xs_use:
                W("sync", prev_xs_use[b])
            ld = prog.dma("sync", sQ,
                          xs_big[b][:, :].rearrange("p (f c) -> p f c", c=TCW),
                          agx_out.ap()[contrib * D:(contrib + 1) * D, ccols]
                          .rearrange("(f p) c -> p f c", p=P))
            W("tensor", ld)
            for tk in range(4):
                bank = pb[4 + (tk & 1)]
                if (tk & 1) in prev_vb:
                    W("tensor", prev_vb[tk & 1])
                for f in range(FC):
                    mm(bank[:, :], xs_big[b][:, f * TCW + tk * P: f * TCW + tk * P + P],
                       wv_b[f][:, :], start=(f == 0), stop=(f == FC - 1))
                md = t_mark()
                W("scalar", md)
                vt = s_op(lambda s, tc=tc, tk=tk, bank=bank: s.activation(
                    v_all[tc * 4 + tk][:, :].rearrange("p (h c) -> p h c", c=65)[:, :, 0:64],
                    bank[:, :].rearrange("p (h c) -> p h c", c=64), AF.Copy))
                prev_vb[tk & 1] = vt
            prev_xs_use[b] = (sT, sT.n)
        for vt_i in range(32):
            v_op(lambda v, vt_i=vt_i: v.memset(
                v_all[vt_i][:, :].rearrange("p (h c) -> p h c", c=65)[:, :, 64:65], 1.0))
        vones = (sV, sV.n)
        eo_done = (sG, sG.n)
        W("sync", eo_done)
        W("sync", (sT, sT.n))
        W("sync", (sV, sV.n))
        W("sync", (sA, sA.n))
        for j in range(8):
            esrc = qe_sc if j < 4 else ke_sc
            osrc = qo_sc if j < 4 else ko_sc
            e0 = 64 * (j % 4)
            prog.dma("sync", sQ, qk_all[j][0:32, :], esrc[e0:e0 + 32, :])
            prog.dma("sync", sQ, qk_all[j][32:64, :], osrc[e0:e0 + 32, :])
            prog.dma("sync", sQ, qk_all[j][64:96, :], esrc[e0 + 32:e0 + 64, :])
            prog.dma("sync", sQ, qk_all[j][96:128, :], osrc[e0 + 32:e0 + 64, :])
        qk_ready = (sQ, sQ.n)
        W("scalar", qk_ready)
        W("tensor", qk_ready)
        W("vector", qk_ready)

        for src_k, base in ((0, 0), (4, 64)):
            for h in range(NH):
                tile = qk_all[src_k + h // 2]
                ro = 64 * (h & 1)
                for t4 in range(NTA):
                    cs = slice(t4 * TCW, (t4 + 1) * TCW)
                    sqd = s_op(lambda s, tile=tile, ro=ro, cs=cs: s.activation(
                        tmpa[ro:ro + 64, :], tile[ro:ro + 64, cs], AF.Square))
                    W("tensor", sqd)
                    mm(pb[6][0:1, :], onesc[ro:ro + 64, 0:1], tmpa[ro:ro + 64, :],
                       start=True, stop=True)
                    nm = t_mark()
                    W("vector", nm)
                    if src_k == 0:
                        vv = v_op(lambda v, h=h, t4=t4: v.tensor_reduce(
                            nq_row[0:1, h * 8 + t4:h * 8 + t4 + 1], pb[6][0:1, :], AX.X, ALU.max))
                    else:
                        vv = v_op(lambda v, h=h, t4=t4: v.tensor_reduce(
                            rowtmp[0:1, 64 + h:64 + h + 1] if t4 else nq_row[0:1, 64 + h:64 + h + 1],
                            pb[6][0:1, :], AX.X, ALU.max))
                        if t4:
                            vv = v_chain(lambda v, h=h: v.tensor_tensor(
                                nq_row[0:1, 64 + h:64 + h + 1], nq_row[0:1, 64 + h:64 + h + 1],
                                rowtmp[0:1, 64 + h:64 + h + 1], ALU.max))
                    W("tensor", vv)
                    W("scalar", vv)
        from concourse.bass import broadcast_tensor_aps
        for h in range(NH):
            a_ap, b_ap = broadcast_tensor_aps(nq_row[0:1, h * 8:h * 8 + 8],
                                              nq_row[0:1, 64 + h:64 + h + 1])
            v_chain(lambda v, a_ap=a_ap, b_ap=b_ap: v.tensor_tensor(
                a_ap, a_ap, b_ap, ALU.mult))
        nv = (sV, sV.n)
        W("scalar", nv)
        sq2 = s_op(lambda s: s.activation(nq_row[0:1, 0:64], nq_row[0:1, 0:64], AF.Sqrt))
        W("vector", sq2)
        v_chain(lambda v: v.tensor_scalar(nq_row[0:1, 0:64], nq_row[0:1, 0:64],
                                          -1.0 / 64.0, None, ALU.mult))
        bs = (sV, sV.n)
        W("gpsimd", bs)
        b1 = prog.dma("gpsimd", sG, brow_sc[0:1, 0:64], nq_row[0:1, 0:64])
        W("gpsimd", b1)
        b2 = prog.dma("gpsimd", sG, mrep[:, 0:64], brow_sc[0:1, 0:64].broadcast_to([P, 64]))
        W("scalar", b2)
        W("vector", vones)
        W("tensor", vones)

        last_exp_on_bank = {}
        prev_ao = [None]
        for h in range(NH):
            ktile = qk_all[4 + h // 2]
            qtile = qk_all[h // 2]
            ro = 64 * (h & 1)
            for qt in range(NTA):
                qcols = slice(qt * TCW, (qt + 1) * TCW)
                av_bank = pb[2 + (qt & 1)]
                nkc = 4 * qt + 4
                prev_avm = None
                for kc in range(nkc):
                    sbk = kc & 1
                    if sbk in last_exp_on_bank:
                        W("tensor", last_exp_on_bank[sbk])
                    mm(pb[sbk][:, :], ktile[ro:ro + 64, kc * P:(kc + 1) * P],
                       qtile[ro:ro + 64, qcols], start=True, stop=True)
                    md = t_mark()
                    W("scalar", md)
                    eb = kc & 1
                    if prev_avm is not None:
                        W("scalar", prev_avm)
                    col = h * 8 + qt
                    ex = s_op(lambda s, sbk=sbk, eb=eb, col=col: s.activation(
                        expS[eb][:, :], pb[sbk][:, :], AF.Exp, scale=0.125,
                        bias=mrep[:, col:col + 1]))
                    last_exp_on_bank[sbk] = ex
                    dep = ex
                    if kc >= 4 * qt:
                        off = kc - 4 * qt
                        W("vector", ex)
                        dep = v_op(lambda v, eb=eb, off=off: v.tensor_tensor(
                            expS[eb][:, :], expS[eb][:, :],
                            masks_t[:, off * TCW:(off + 1) * TCW], ALU.mult))
                        W("tensor", dep)
                        last_exp_on_bank[sbk] = dep
                    else:
                        W("tensor", ex)
                    mm(av_bank[0:65, :], v_all[kc][:, h * 65:(h + 1) * 65],
                       expS[eb][:, :], start=(kc == 0), stop=(kc == nkc - 1))
                    prev_avm = t_mark()
                W("vector", prev_avm)
                v_op(lambda v, av_bank=av_bank: v.reciprocal(tmpb[0:1, 0:TCW], av_bank[64:65, :]))
                rc = v_chain(lambda v: v.tensor_copy(recip_row[0:1, :], tmpb[0:1, 0:TCW]))
                W("tensor", rc)
                mm(pb[6][0:64, :], onesrb[0:1, 0:64], recip_row[0:1, :], start=True, stop=True)
                rep = t_mark()
                W("vector", rep)
                rcp = v_op(lambda v: v.tensor_copy(tmpa[0:64, :], pb[6][0:64, :]))
                W("tensor", rcp)
                if prev_ao[0] is not None:
                    W("vector", prev_ao[0])
                ao = v_op(lambda v, h=h, qcols=qcols, av_bank=av_bank: v.tensor_tensor(
                    ao_stg[64 * (h & 1):64 * (h & 1) + 64, qcols],
                    av_bank[0:64, :], tmpa[0:64, :], ALU.mult))
                W("tensor", ao)
            if h & 1:
                aod = (sV, sV.n)
                W("gpsimd", aod)
                dst_r = (h - 1) * 64
                ad = prog.dma("gpsimd", sG, aga_in.ap()[dst_r:dst_r + P, :], ao_stg[:, :])
                prev_ao[0] = ad
        agc2 = prog.inst("gpsimd", sC, lambda g: g.collective_compute(
            "AllGather", ALU.bypass, replica_groups=[[0, 1], [2, 3], [4, 5], [6, 7]],
            ins=[aga_in.ap()], outs=[aga_out.ap()]))

        av_all_done = (sT, sT.n)
        W("sync", av_all_done)
        prev_wo = [None]
        for k in range(16):
            wb = k & 1
            if prev_wo[0] is not None and k >= 2:
                W("sync", prev_wo[0])
            ld = prog.dma("sync", sQ, wo_raw[wb][:, :], woT4[k, :, :])
            W("vector", ld)
            cv = v_op(lambda v, k=k, wb=wb: v.tensor_copy(wo16[k][:, :], wo_raw[wb][:, :]))
            prev_wo[0] = cv
        W("sync", agc2)
        wo_done = (sV, sV.n)
        W("tensor", wo_done)
        prev_ob = {}
        for tcw in range(NT):
            for sl in range(4):
                half, contrib = sl // 2, sl % 2
                cols = slice(half * T + tcw * TCW, half * T + (tcw + 1) * TCW)
                for hf in range(4):
                    prog.dma("sync", sQ, agaS[sl * 4 + hf][:, :],
                             aga_out.ap()[contrib * 512 + hf * P: contrib * 512 + (hf + 1) * P, cols])
            agld = (sQ, sQ.n)
            W("tensor", agld)
            for fo in range(FC):
                bank = pb[6 + (fo & 1)]
                if (fo & 1) in prev_ob:
                    W("tensor", prev_ob[fo & 1])
                first = True
                for k in range(16):
                    mm(bank[:, :], wo16[k][:, fo * P:(fo + 1) * P], agaS[k][:, :],
                       start=first, stop=(k == 15))
                    first = False
                md = t_mark()
                W("vector", md)
                res = v_op(lambda v, fo=fo, tcw=tcw, bank=bank: v.tensor_tensor(
                    x8[fo][:, tcw * TCW:(tcw + 1) * TCW],
                    x8[fo][:, tcw * TCW:(tcw + 1) * TCW], bank[:, :], ALU.add))
                prev_ob[fo & 1] = res
            W("sync", (sT, sT.n))
        sync_all()

    if phases != "nop":
        emit_ffn(0)
    if phases in ("f1a", "full"):
        emit_attention()
    if phases == "full":
        emit_ffn(1)

    fin = (sV, sV.n)
    W("gpsimd", fin)
    for i in range(FC):
        prog.dma("gpsimd", sO, y_out[i * P:(i + 1) * P, :], x8[i][:, :])
    if phases == "p2x":
        prog.dma("gpsimd", sO, dbg_out[:, 0:96], mx[:, 0:96])
    elif phases == "p2y":
        prog.dma("gpsimd", sO, dbg_out[:, 0:96], mx[:, 96:192])
    else:
        prog.dma("gpsimd", sO, dbg_out[:, 0:64], scales[:, :])
        prog.dma("gpsimd", sO, dbg_out[0:1, 64:96], row32[:, :])

    from contextlib import ExitStack
    with ExitStack() as ctx:
        block = ctx.enter_context(nc.Block())

        @block.sync
        def _(eng):
            for fn in prog.q["sync"]:
                fn(eng)

        @block.tensor
        def _(eng):
            for fn in prog.q["tensor"]:
                fn(eng)

        @block.vector
        def _(eng):
            for fn in prog.q["vector"]:
                fn(eng)

        @block.scalar
        def _(eng):
            for fn in prog.q["scalar"]:
                fn(eng)

        @block.gpsimd
        def _(eng):
            for fn in prog.q["gpsimd"]:
                fn(eng)
            eng.wait_ge(sO.h, sO.n)

    return nc


def _host_inputs(inputs, core):
    pair, parity = core // 2, core & 1
    x = np.asarray(inputs["L0_attn_x"], np.float32)
    Wq = np.asarray(inputs["L0_attn_Wq"], np.float32)
    Wk = np.asarray(inputs["L0_attn_Wk"], np.float32)
    Wv = np.asarray(inputs["L0_attn_Wv"], np.float32)
    Wo = np.asarray(inputs["L0_attn_Wo"], np.float32)
    wg = np.asarray(inputs["L0_ffn_w_gate"], np.float32)
    wu = np.asarray(inputs["L0_ffn_w_up"], np.float32)
    wd = np.asarray(inputs["L0_ffn_w_down"], np.float32)

    xc = np.ascontiguousarray(x[pair, parity * T:(parity + 1) * T].T)

    rows = []
    for chunk in range(4):
        odd = 1 if chunk >= 2 else 0
        hbase = 4 * (chunk % 2)
        for hh in range(4):
            head = parity * 8 + hbase + hh
            for i in range(32):
                rows.append(head * 64 + 2 * i + odd)
    rows = np.array(rows)
    vrows = np.arange(parity * 512, (parity + 1) * 512)
    wqkvT = np.ascontiguousarray(
        np.concatenate([Wq[rows].T, Wk[rows].T, Wv[vrows].T], axis=1))

    WoT = np.ascontiguousarray(Wo.T)
    tiles = []
    for sl in range(4):
        half, contrib = sl // 2, sl % 2
        use = (half == parity)
        blk = WoT[contrib * 512:(contrib + 1) * 512] if use else np.zeros((512, D), np.float32)
        for hf in range(4):
            tiles.append(blk[hf * P:(hf + 1) * P])
    woT4 = np.ascontiguousarray(np.stack(tiles))

    inv = 10000.0 ** (-np.arange(32, dtype=np.float64) * 2.0 / DK)
    tpos = np.arange(TA, dtype=np.float32)
    ang = (tpos[None, :] * inv[:, None].astype(np.float32)).astype(np.float32)
    cos1 = np.cos(ang.astype(np.float64)).astype(np.float32)
    sin1 = np.sin(ang.astype(np.float64)).astype(np.float32)
    costab = np.ascontiguousarray(np.tile(cos1, (4, 1)))
    sintab = np.ascontiguousarray(np.tile(sin1, (4, 1)))

    kk = np.arange(P)[:, None]
    qq = np.arange(TCW)[None, :]
    m = np.concatenate([(kk + off * P <= qq) for off in range(4)], axis=1).astype(BFNP)

    return {
        "x_in": xc,
        "wqkvT": wqkvT,
        "woT4": woT4,
        "wupT": np.ascontiguousarray(wu.T),
        "wgateT": np.ascontiguousarray(wg.T),
        "wdownT": np.ascontiguousarray(wd.T),
        "wup_n": np.ascontiguousarray(wu),
        "wgate_n": np.ascontiguousarray(wg),
        "wdown_n": np.ascontiguousarray(wd),
        "costab": costab,
        "sintab": sintab,
        "masks_in": np.ascontiguousarray(m),
        "onesc_in": np.concatenate([np.ones((P, 1), np.float32), np.full((P, 1), 1e-5, np.float32), np.full((P, 1), MAGIC, np.float32), np.ones((P, 5), np.float32)], axis=1),
        "onesr_in": np.ones((1, P), np.float32),
        "id_in": np.eye(P, dtype=np.float32),
    }


def kernel(**inputs):
    nc = build_program(_PHASES)
    in_maps = [_host_inputs(inputs, c) for c in range(8)]
    res = run_bass_kernel_spmd(nc, in_maps, core_ids=list(range(8)))
    out = np.zeros((4, TA, D), np.float32)
    for c in range(8):
        pair, parity = c // 2, c & 1
        out[pair, parity * T:(parity + 1) * T] = res.results[c]["y_out"].T
    return out


# version 03 (final)
# speedup vs baseline: 1.1476x; 1.1476x over previous
import numpy as np
import ml_dtypes
import concourse.bass as bass
import concourse.mybir as mybir
from concourse.bass_utils import run_bass_kernel_spmd

F32 = mybir.dt.float32
BF16 = mybir.dt.bfloat16
I32 = mybir.dt.int32
AF = mybir.ActivationFunctionType
ALU = mybir.AluOpType
AX = mybir.AxisListType
BFNP = ml_dtypes.bfloat16

P = 128
D = 1024
FC = D // P
T = 2048
TA = 4096
I3 = 3072
IC = I3 // P
TCW = 512
NT = T // TCW
NTA = TA // TCW
NH = 8
DK = 64
QMAX = 32767.0
FLOOR = 32767.0e-12
MAGIC = 12582912.0

_PHASES = "full"


class Sem:
    def __init__(self, nc, name):
        self.h = nc.alloc_semaphore(name)
        self.n = 0
        self.dma = False


class Prog:
    ENGINES = ("sync", "tensor", "vector", "scalar", "gpsimd")

    def __init__(self, nc):
        self.nc = nc
        self.q = {e: [] for e in self.ENGINES}

    def emit(self, e, fn):
        self.q[e].append(fn)

    def wait(self, e, snap):
        sem, val = snap
        assert val <= sem.n
        if sem.dma:
            val = sem.n
        h = sem.h
        self.emit(e, lambda eng: eng.wait_ge(h, val))

    def dma(self, e, sem, out, in_):
        h = sem.h
        sem.dma = True
        self.emit(e, lambda eng: eng.dma_start(out=out, in_=in_).then_inc(h, 16))
        sem.n += 16
        return (sem, sem.n)

    def inst(self, e, sem, fn):
        if sem is None:
            self.emit(e, fn)
            return None
        h = sem.h
        self.emit(e, lambda eng: fn(eng).then_inc(h, 1))
        sem.n += 1
        return (sem, sem.n)


def build_program(phases=_PHASES):
    nc = bass.Bass("TRN2", target_bir_lowering=False, debug=False, detect_race_conditions=False)

    x_in = nc.dram_tensor("x_in", [D, T], F32, kind="ExternalInput").ap()
    wqkvT = nc.dram_tensor("wqkvT", [D, 1536], F32, kind="ExternalInput").ap()
    woT4 = nc.dram_tensor("woT4", [16, P, D], F32, kind="ExternalInput").ap()
    wup_n = nc.dram_tensor("wup_n", [I3, D], F32, kind="ExternalInput").ap()
    wgate_n = nc.dram_tensor("wgate_n", [I3, D], F32, kind="ExternalInput").ap()
    wdown_n = nc.dram_tensor("wdown_n", [D, I3], F32, kind="ExternalInput").ap()
    costab = nc.dram_tensor("costab", [P, TA], F32, kind="ExternalInput").ap()
    sintab = nc.dram_tensor("sintab", [P, TA], F32, kind="ExternalInput").ap()
    masks_in = nc.dram_tensor("masks_in", [P, 4 * TCW], BF16, kind="ExternalInput").ap()
    onesc_in = nc.dram_tensor("onesc_in", [P, 8], F32, kind="ExternalInput").ap()
    onesr_in = nc.dram_tensor("onesr_in", [1, P], F32, kind="ExternalInput").ap()
    id_in = nc.dram_tensor("id_in", [P, P], F32, kind="ExternalInput").ap()
    y_out = nc.dram_tensor("y_out", [D, T], F32, kind="ExternalOutput").ap()
    dbg_out = nc.dram_tensor("dbg_out", [P, 96], F32, kind="ExternalOutput").ap()

    u_sc = nc.dram_tensor("u_sc", [I3, T], BF16, kind="Internal").ap()
    g_sc = nc.dram_tensor("g_sc", [I3, T], BF16, kind="Internal").ap()
    qe_sc = nc.dram_tensor("qe_sc", [256, TA], BF16, kind="Internal").ap()
    qo_sc = nc.dram_tensor("qo_sc", [256, TA], BF16, kind="Internal").ap()
    ke_sc = nc.dram_tensor("ke_sc", [256, TA], BF16, kind="Internal").ap()
    ko_sc = nc.dram_tensor("ko_sc", [256, TA], BF16, kind="Internal").ap()
    dqkind = "ExternalOutput" if phases == "dqx" else "Internal"
    wupq_sc = nc.dram_tensor("wupq_sc", [D, I3], BF16, kind=dqkind).ap()
    wgateq_sc = nc.dram_tensor("wgateq_sc", [D, I3], BF16, kind=dqkind).ap()
    wdownq_sc = nc.dram_tensor("wdownq_sc", [I3, D], BF16, kind="Internal").ap()
    bar_in = nc.dram_tensor("bar_in", [1, 8], F32, kind="Internal")
    bar_out = nc.dram_tensor("bar_out", [1, 8], F32, kind="Internal")
    scal_sc = nc.dram_tensor("scal_sc", [1, 8], F32, kind="Internal").ap()
    brow_sc = nc.dram_tensor("brow_sc", [1, 64], F32, kind="Internal").ap()
    agx_in = nc.dram_tensor("agx_in", [D, T], BF16, kind="Internal")
    agx_out = nc.dram_tensor("agx_out", [2 * D, T], BF16, kind="Internal")
    aga_in = nc.dram_tensor("aga_in", [512, TA], BF16, kind="Internal")
    aga_out = nc.dram_tensor("aga_out", [2 * 512, TA], BF16, kind="Internal")

    SB0 = (nc.sbuf_base + 31) // 32 * 32

    def alloc(name, shape, dt, at):
        return nc.alloc_sbuf_tensor_at(name, shape, dt, offset=SB0 + at)

    x8 = [alloc(f"x{i}", [P, T], F32, i * 8192) for i in range(FC)]
    RES = 64 * 1024
    scales = alloc("scales", [P, 64], F32, RES)
    onesc = alloc("onesc", [P, 8], F32, RES + 256)
    onescb = alloc("onescb", [P, 8], BF16, RES + 288)
    onesr = alloc("onesr", [1, P], F32, RES + 320)
    onesrb = alloc("onesrb", [1, P], BF16, RES + 832)
    row32 = alloc("row32", [1, 32], F32, RES + 1088)
    rowtmp = alloc("rowtmp", [1, P], F32, RES + 1216)
    masks_t = alloc("masks_t", [P, 4 * TCW], BF16, RES + 1728)
    mu_row = alloc("mu_row", [1, T], F32, RES + 5824)
    r_row = alloc("r_row", [1, T], F32, RES + 14016)
    mx = alloc("mx", [P, 200], F32, RES + 22208)
    tmpa = alloc("tmpa", [P, TCW], F32, RES + 23008)
    tmpb = alloc("tmpb", [P, TCW], F32, RES + 25056)
    xb2 = [alloc(f"xb2_{i}", [P, TCW], BF16, RES + 27104 + i * 1024) for i in range(2)]
    x2b = alloc("x2b", [P, TCW], BF16, RES + 29152)
    id128 = alloc("id128", [P, P], F32, RES + 30176)

    WORK = 94 * 1024 + 256
    BIG = WORK + 56 * 1024
    assert BIG + 56 * 1024 <= 212831

    xn8 = [alloc(f"xn{i}", [P, T], BF16, WORK + i * 4096) for i in range(FC)]
    wn_raw = alloc("wn_raw", [P, I3], F32, WORK + 0)
    sn_t = alloc("sn_t", [P, 96], F32, WORK + 12 * 1024)
    sn_r = alloc("sn_r", [P, 96], F32, WORK + 12 * 1024 + 512)
    sn_s = alloc("sn_s", [P, 96], F32, WORK + 12 * 1024 + 1024)
    dq_raw = [alloc(f"dq_raw{i}", [P, I3], F32, WORK + 14 * 1024 + i * 12 * 1024) for i in range(2)]
    stage_t = alloc("stage_t", [P, I3], BF16, WORK + 26 * 1024)
    p2_stg = [alloc(f"p2_stg{i}", [P, T], BF16, WORK + 38 * 1024 + i * 4096) for i in range(2)]
    p2_w = [alloc(f"p2_w{i}", [P, FC * P], BF16, WORK + 46 * 1024 + i * 2048) for i in range(2)]

    xq8 = [alloc(f"xq{i}", [P, T], BF16, BIG + i * 4096) for i in range(FC)]
    g2b = [alloc(f"g2b{i}", [P, T], BF16, WORK + i * 4096) for i in range(14)] + \
          [alloc(f"g2b{i}", [P, T], BF16, BIG + (i - 14) * 4096) for i in range(14, IC)]
    TH = T // 2
    tmpA2 = alloc("tmpA2", [P, TH], F32, BIG + 40 * 1024)
    tmpB2 = alloc("tmpB2", [P, TH], F32, BIG + 44 * 1024)
    tmpf1 = alloc("tmpf1", [P, TH], F32, BIG + 48 * 1024)
    strm = [alloc(f"strm{i}", [P, TH], BF16, BIG + 52 * 1024 + i * 2048) for i in range(2)]
    xqi = alloc("xqi", [P, T], F32, BIG + 32 * 1024)
    tq1 = alloc("tq1", [P, I3], F32, BIG + 32 * 1024)
    tq2 = alloc("tq2", [P, I3], F32, BIG + 44 * 1024)
    p5_w = alloc("p5_w", [P, IC * P], BF16, BIG + 46 * 1024)

    xs_big = [alloc(f"xs_big{i}", [P, TA], BF16, WORK + i * 8192) for i in range(2)]
    cos_t = alloc("cos_t", [P, TA], F32, WORK + 16 * 1024)
    sin_t = alloc("sin_t", [P, TA], F32, WORK + 32 * 1024)
    rope_t1 = alloc("rope_t1", [P, TCW], F32, WORK + 48 * 1024)
    rope_t2 = alloc("rope_t2", [P, TCW], F32, WORK + 50 * 1024)
    eo_stg = [alloc(f"eo_stg{i}", [P, TCW], BF16, WORK + 52 * 1024 + i * 1024) for i in range(2)]
    qk_all = [alloc(f"qk_all{i}", [P, TA], BF16, WORK + i * 8192) for i in range(7)] + \
             [alloc("qk_all7", [P, TA], BF16, BIG + 45 * 1024)]
    wo16 = [alloc(f"wo16_{i}", [P, D], BF16, WORK + i * 2048) for i in range(16)]

    wqk_b = [alloc(f"wqk_b{i}", [P, 1024], BF16, BIG + i * 2048) for i in range(FC)]
    wv_b = [alloc(f"wv_b{i}", [P, 512], BF16, WORK + 16 * 1024 + i * 1024) for i in range(FC)]
    wqkv_stage = alloc("wqkv_stage", [P, 1536], F32, BIG + 16 * 1024)
    v_all = [alloc(f"v_all{i}", [P, NH * 65], BF16, BIG + i * 1056) for i in range(32)]
    ao_stg = alloc("ao_stg", [P, TA], BF16, BIG + 33 * 1024)
    expS = [alloc(f"expS{i}", [P, TCW], BF16, BIG + 41 * 1024 + i * 1024) for i in range(2)]
    mrep = alloc("mrep", [P, 80], F32, BIG + 43 * 1024)
    nq_row = alloc("nq_row", [1, 128], F32, BIG + 43 * 1024 + 320)
    recip_row = alloc("recip_row", [1, TCW], BF16, BIG + 44 * 1024)
    agaS = [alloc(f"agaS{i}", [P, TCW], BF16, BIG + i * 1024) for i in range(16)]
    wo_raw = [alloc(f"wo_raw{i}", [P, D], F32, BIG + 16 * 1024 + i * 4096) for i in range(2)]

    pb = [nc.alloc_psum_tensor(f"pb{i}", [P, TCW], F32) for i in range(8)]
    nc.const_aps.register(F32, MAGIC, onesc[:, 2:3])
    nc.const_aps.register(F32, 1e-5, onesc[:, 1:2])

    prog = Prog(nc)
    sQ = Sem(nc, "sQ")
    sG = Sem(nc, "sG")
    sT = Sem(nc, "sT")
    sV = Sem(nc, "sV")
    sA = Sem(nc, "sA")
    sC = Sem(nc, "sC")
    sO = Sem(nc, "sO")

    def mm(psum_ap, lhsT, rhs, start, stop):
        prog.emit("tensor", lambda t, a=psum_ap, b=lhsT, c=rhs, s1=start, s2=stop:
                  t.matmul(a, b, c, start=s1, stop=s2))

    def t_mark():
        last = prog.q["tensor"].pop()
        h = sT.h
        prog.q["tensor"].append(lambda t, last=last: last(t).then_inc(h, 1))
        sT.n += 1
        return (sT, sT.n)

    def v_op(fn):
        return prog.inst("vector", sV, fn)

    def v_chain(fn):
        W("vector", (sV, sV.n))
        return prog.inst("vector", sV, fn)

    def s_op(fn):
        return prog.inst("scalar", sA, fn)

    W = prog.wait

    def sync_all():
        snaps = [(sT, sT.n), (sV, sV.n), (sA, sA.n), (sQ, sQ.n), (sG, sG.n)]
        for e in ("sync", "tensor", "vector", "scalar", "gpsimd"):
            for sn in snaps:
                W(e, sn)

    for i in range(FC):
        prog.dma("sync", sQ, x8[i][:, :], x_in[i * P:(i + 1) * P, :])
    prog.dma("sync", sQ, masks_t[:, :], masks_in[:, :])
    prog.dma("sync", sQ, onesc[:, :], onesc_in[:, :])
    prog.dma("sync", sQ, onesr[:, :], onesr_in[:, :])
    prog.dma("sync", sQ, id128[:, :], id_in[:, :])
    loads = (sQ, sQ.n)
    W("vector", loads)
    v_op(lambda v: v.tensor_copy(onescb[:, :], onesc[:, :]))
    v_op(lambda v: v.tensor_copy(onesrb[:, :], onesr[:, :]))
    v_op(lambda v: v.memset(row32[:, :], 0.0))
    v_op(lambda v: v.memset(rowtmp[:, :], 0.0))
    sync_all()

    def barrier_max():
        vs = (sV, sV.n)
        W("gpsimd", vs)
        w = prog.dma("gpsimd", sG, bar_in.ap()[0:1, 0:8], row32[0:1, 0:8])
        W("gpsimd", w)
        c = prog.inst("gpsimd", sC, lambda g: g.collective_compute(
            "AllReduce", ALU.max, replica_groups=[list(range(8))],
            ins=[bar_in.ap()], outs=[bar_out.ap()]))
        W("gpsimd", c)
        r = prog.dma("gpsimd", sG, row32[0:1, 8:16], bar_out.ap()[0:1, 0:8])
        W("vector", r)
        return r

    def replicate(src_col, dst_col, n):
        vs = (sV, sV.n)
        W("gpsimd", vs)
        w = prog.dma("gpsimd", sG, scal_sc[0:1, 0:n], row32[0:1, src_col:src_col + n])
        W("gpsimd", w)
        r = prog.dma("gpsimd", sG, scales[:, dst_col:dst_col + n],
                     scal_sc[0:1, 0:n].broadcast_to([P, n]))
        return r

    def colmax(lo, n, col):
        v_chain(lambda v: v.tensor_reduce(mx[:, 196:197], mx[:, lo:lo + n], AX.X,
                                          ALU.max, apply_absolute_value=True))
        vd = (sV, sV.n)
        W("gpsimd", vd)
        d = prog.dma("gpsimd", sG, rowtmp[0:1, 0:P], mx[:, 196:197])
        W("vector", d)
        v_chain(lambda v: v.tensor_reduce(row32[0:1, col:col + 1], rowtmp[0:1, 0:P],
                                          AX.X, ALU.max, apply_absolute_value=True))

    def emit_ln(dst_fn, do_max, nt=NT):
        last_rep = [None]
        for t in range(nt):
            cs = slice(t * TCW, (t + 1) * TCW)
            for i in range(FC):
                b = i & 1
                c1 = v_op(lambda v, i=i, cs=cs, b=b: v.tensor_copy(xb2[b][:, :], x8[i][:, cs]))
                c2 = s_op(lambda s, i=i, cs=cs: s.activation(x2b[:, :], x8[i][:, cs], AF.Square))
                W("tensor", c1)
                W("tensor", c2)
                mm(pb[0][0:1, :], onescb[:, 0:1], xb2[b][:, :], start=(i == 0), stop=(i == FC - 1))
                mm(pb[1][0:1, :], onescb[:, 0:1], x2b[:, :], start=(i == 0), stop=(i == FC - 1))
                ms = t_mark()
                if i < FC - 1:
                    W("vector", ms)
                    W("scalar", ms)
            tdone = (sT, sT.n)
            W("vector", tdone)
            v_op(lambda v, cs=cs: v.tensor_scalar(mu_row[0:1, cs], pb[0][0:1, :], 1.0 / D, None, ALU.mult))
            v_op(lambda v, cs=cs: v.tensor_scalar(r_row[0:1, cs], pb[1][0:1, :], 1.0 / D, None, ALU.mult))
            v_chain(lambda v, cs=cs: v.tensor_tensor(tmpa[0:1, :], mu_row[0:1, cs], mu_row[0:1, cs], ALU.mult))
            vv = v_chain(lambda v, cs=cs: v.tensor_tensor(r_row[0:1, cs], r_row[0:1, cs], tmpa[0:1, :], ALU.subtract))
            W("scalar", vv)
            sq = s_op(lambda s, cs=cs: s.activation(r_row[0:1, cs], r_row[0:1, cs], AF.Sqrt, bias=1e-5))
            W("vector", sq)
            rd = v_op(lambda v, cs=cs: v.reciprocal(r_row[0:1, cs], r_row[0:1, cs]))
            W("tensor", rd)
            if last_rep[0] is not None:
                W("tensor", last_rep[0])
            mm(pb[2][:, :], onesr[0:1, 0:P], mu_row[0:1, cs], start=True, stop=True)
            mm(pb[3][:, :], onesr[0:1, 0:P], r_row[0:1, cs], start=True, stop=True)
            rep = t_mark()
            W("vector", rep)
            for i in range(FC):
                v_op(lambda v, i=i, cs=cs: v.tensor_tensor(tmpa[:, :], x8[i][:, cs], pb[2][:, :], ALU.subtract))
                v_op(lambda v, i=i, cs=cs: v.tensor_tensor(tmpb[:, :], tmpa[:, :], pb[3][:, :], ALU.mult))
                v_op(lambda v, i=i, t=t: v.tensor_copy(dst_fn(i, t), tmpb[:, :]))
                if do_max:
                    v_op(lambda v, i=i, t=t: v.tensor_reduce(
                        mx[:, (t * FC + i):(t * FC + i + 1)], tmpb[:, :], AX.X,
                        ALU.max, apply_absolute_value=True))
            last_rep[0] = (sV, sV.n)

    def emit_dequant2(nat_ap, nchunks, nin, nblk, cache_ap):
        W("sync", (sV, sV.n))
        nkt = nin // P
        raws = [wn_raw, dq_raw[0]]
        prev_raw = {}
        prev_tr = [None]
        prev_m2 = [None]
        prev_cd = [None]
        for c in range(nchunks):
            rb = raws[c & 1]
            if (c & 1) in prev_raw:
                W("sync", prev_raw[c & 1])
            ld = prog.dma("sync", sQ, rb[:, 0:nin], nat_ap[c * P:(c + 1) * P, :])
            W("vector", ld)
            v_op(lambda v, rb=rb, nin=nin, nblk=nblk: v.tensor_reduce(
                sn_t[:, 0:nblk], rb[:, 0:nin].rearrange("p (b k) -> p b k", k=32),
                AX.X, ALU.max, apply_absolute_value=True))
            v_chain(lambda v, nblk=nblk: v.tensor_scalar(sn_t[:, 0:nblk], sn_t[:, 0:nblk],
                                                         7e-12, None, ALU.max))
            v_chain(lambda v, nblk=nblk: v.reciprocal(sn_r[:, 0:nblk], sn_t[:, 0:nblk]))
            v_chain(lambda v, nblk=nblk: v.tensor_scalar(sn_r[:, 0:nblk], sn_r[:, 0:nblk],
                                                         7.0, None, ALU.mult))
            v_chain(lambda v, nblk=nblk: v.tensor_scalar(sn_s[:, 0:nblk], sn_t[:, 0:nblk],
                                                         1.0 / 7.0, None, ALU.mult))
            if prev_tr[0] is not None:
                W("vector", prev_tr[0])
            m1 = v_chain(lambda v, rb=rb, nin=nin, nblk=nblk: v.tensor_tensor(
                tq1[:, 0:nin].rearrange("p (b k) -> p b k", k=32),
                rb[:, 0:nin].rearrange("p (b k) -> p b k", k=32),
                sn_r[:, 0:nblk].unsqueeze(2).broadcast_to([P, nblk, 32]), ALU.mult))
            prev_raw[c & 1] = (sV, sV.n)
            W("scalar", m1)
            if prev_m2[0] is not None:
                W("scalar", prev_m2[0])
            r1 = s_op(lambda s, nin=nin: s.activation(tq2[:, 0:nin], tq1[:, 0:nin],
                                                      AF.Copy, bias=MAGIC))
            W("vector", r1)
            m2 = v_op(lambda v, nin=nin, nblk=nblk: v.scalar_tensor_tensor(
                tq1[:, 0:nin].rearrange("p (b k) -> p b k", k=32),
                tq2[:, 0:nin].rearrange("p (b k) -> p b k", k=32), -MAGIC,
                sn_s[:, 0:nblk].unsqueeze(2).broadcast_to([P, nblk, 32]),
                ALU.add, ALU.mult))
            prev_m2[0] = m2
            W("tensor", m2)
            prev_ev = {}
            for k in range(nkt):
                bank = pb[4 + (k & 1)]
                if (k & 1) in prev_ev:
                    W("tensor", prev_ev[k & 1])
                prog.emit("tensor", lambda t, k=k, bank=bank: t.transpose(
                    bank[:, 0:P], tq1[:, k * P:(k + 1) * P], id128[:, :]))
                tm = t_mark()
                if (k & 1) == 0:
                    W("vector", tm)
                    if prev_cd[0] is not None and k < 2:
                        W("vector", prev_cd[0])
                    ev = v_op(lambda v, k=k, bank=bank: v.tensor_copy(
                        stage_t[:, k * P:(k + 1) * P], bank[:, 0:P]))
                else:
                    W("scalar", tm)
                    if prev_cd[0] is not None and k < 2:
                        W("scalar", prev_cd[0])
                    ev = s_op(lambda s, k=k, bank=bank: s.activation(
                        stage_t[:, k * P:(k + 1) * P], bank[:, 0:P], AF.Copy))
                prev_ev[k & 1] = ev
            prev_tr[0] = tm
            W("gpsimd", prev_ev[0])
            if 1 in prev_ev:
                W("gpsimd", prev_ev[1])
            cd = prog.dma("gpsimd", sG,
                          cache_ap.rearrange("(k p) o -> p k o", p=P)[:, :, c * P:(c + 1) * P],
                          stage_t[:, 0:nin].rearrange("p (k o) -> p k o", o=P))
            prev_cd[0] = cd
        return (sG, sG.n)

    def emit_p2(n_mat=2, n_oc=IC):
        prev_stg = {}
        prev_bank = {}
        for mi, (cache_ap, out_ap) in enumerate(((wupq_sc, u_sc), (wgateq_sc, g_sc))[:n_mat]):
            for oc in range(n_oc):
                wb = oc & 1
                ld = prog.dma("sync", sQ, p2_w[wb][:, :].rearrange("p (f c) -> p f c", c=P),
                              cache_ap.rearrange("(f p) o -> p f o", p=P)[:, :, oc * P:(oc + 1) * P])
                W("tensor", ld)
                sg = oc & 1
                if sg in prev_stg:
                    W("scalar", prev_stg[sg])
                import os
                dbg = os.environ.get("P2DBG", "")
                for tc in range(NT):
                    bank = pb[tc & 1]
                    if (tc & 1) in prev_bank:
                        for sn in prev_bank[tc & 1]:
                            W("tensor", sn)
                    for f in range(FC):
                        mm(bank[:, :], p2_w[wb][:, f * P:(f + 1) * P],
                           xq8[f][:, tc * TCW:(tc + 1) * TCW],
                           start=(f == 0), stop=(f == FC - 1))
                    md = t_mark()
                    W("scalar", md)
                    if dbg == "vevac":
                        ev = v_op(lambda v, sg=sg, tc=tc, bank=bank: v.tensor_copy(
                            p2_stg[sg][:, tc * TCW:(tc + 1) * TCW], bank[:, :]))
                    elif dbg == "noevac":
                        ev = v_op(lambda v, sg=sg, tc=tc: v.memset(
                            p2_stg[sg][:, tc * TCW:(tc + 1) * TCW], 0.0))
                    else:
                        ev = s_op(lambda s, sg=sg, tc=tc, bank=bank: s.activation(
                            p2_stg[sg][:, tc * TCW:(tc + 1) * TCW], bank[:, :], AF.Copy))
                    W("vector", md)
                    W("vector", ev)
                    mxc = mi * 96 + oc * 4 + tc
                    if dbg in ("nored", "noevac"):
                        vr = v_op(lambda v, mxc=mxc: v.memset(mx[:, mxc:mxc + 1], 0.0))
                    else:
                        vr = v_op(lambda v, bank=bank, mxc=mxc: v.tensor_reduce(
                            mx[:, mxc:mxc + 1], bank[:, :], AX.X, ALU.max, apply_absolute_value=True))
                    prev_bank[tc & 1] = (ev, vr)
                evd = (sA, sA.n)
                W("gpsimd", evd)
                st = prog.dma("gpsimd", sG, out_ap[oc * P:(oc + 1) * P, :], p2_stg[sg][:, :])
                prev_stg[sg] = st
                W("sync", (sT, sT.n))

    def emit_p3():
        spill_done = (sG, sG.n)
        W("sync", spill_done)
        W("sync", (sA, sA.n))
        W("sync", (sV, sV.n))
        prev = {}
        for ic in range(IC):
            for hh in range(2):
                b = (2 * ic + hh) & 1
                ts_ = slice(hh * TH, (hh + 1) * TH)
                if b in prev:
                    W("sync", prev[b])
                ld = prog.dma("sync", sQ, strm[b][:, :], g_sc[ic * P:(ic + 1) * P, ts_])
                W("scalar", ld)
                a1 = s_op(lambda s, b=b: s.activation(tmpA2[:, :], strm[b][:, :], AF.Copy,
                                                      scale=scales[:, 5:6], bias=MAGIC))
                a2 = s_op(lambda s: s.activation(tmpf1[:, :], tmpA2[:, :], AF.Sigmoid,
                                                 scale=scales[:, 4:5], bias=scales[:, 6:7]))
                a3 = s_op(lambda s: s.activation(tmpB2[:, :], tmpf1[:, :], AF.Copy,
                                                 scale=65536.0, bias=MAGIC))
                W("vector", a3)
                sc_ = v_op(lambda v: v.tensor_scalar(tmpB2[:, :], tmpB2[:, :], -MAGIC, 65535.0,
                                                     ALU.add, ALU.min))
                g2 = v_op(lambda v, ic=ic, ts_=ts_: v.scalar_tensor_tensor(
                    g2b[ic][:, ts_], tmpA2[:, :], -MAGIC, tmpB2[:, :], ALU.add, ALU.mult))
                vr = v_op(lambda v, ic=ic, hh=hh, ts_=ts_: v.tensor_reduce(
                    mx[:, 2 * ic + hh:2 * ic + hh + 1], g2b[ic][:, ts_],
                    AX.X, ALU.max, apply_absolute_value=True))
                W("scalar", g2)
                prev[b] = vr

    def emit_p4():
        W("sync", (sA, sA.n))
        W("sync", (sV, sV.n))
        prev = {}
        for ic in range(IC):
            for hh in range(2):
                b = (2 * ic + hh) & 1
                ts_ = slice(hh * TH, (hh + 1) * TH)
                if b in prev:
                    W("sync", prev[b])
                ld = prog.dma("sync", sQ, strm[b][:, :], u_sc[ic * P:(ic + 1) * P, ts_])
                W("scalar", ld)
                a1 = s_op(lambda s, b=b: s.activation(tmpA2[:, :], strm[b][:, :], AF.Copy,
                                                      scale=scales[:, 3:4], bias=MAGIC))
                a2 = s_op(lambda s, ic=ic, ts_=ts_: s.activation(tmpB2[:, :], g2b[ic][:, ts_], AF.Copy,
                                                                scale=scales[:, 9:10], bias=MAGIC))
                W("vector", a2)
                g2q = v_op(lambda v: v.tensor_scalar(tmpB2[:, :], tmpB2[:, :], -MAGIC, None, ALU.add))
                ov = v_op(lambda v, ic=ic, ts_=ts_: v.scalar_tensor_tensor(
                    g2b[ic][:, ts_], tmpA2[:, :], -MAGIC, tmpB2[:, :], ALU.add, ALU.mult))
                vr = v_op(lambda v, ic=ic, hh=hh, ts_=ts_: v.tensor_reduce(
                    mx[:, 2 * ic + hh:2 * ic + hh + 1], g2b[ic][:, ts_],
                    AX.X, ALU.max, apply_absolute_value=True))
                W("scalar", ov)
                prev[b] = vr

    def emit_p5():
        for ic in range(IC):
            for hh in range(2):
                ts_ = slice(hh * TH, (hh + 1) * TH)
                a1 = s_op(lambda s, ic=ic, ts_=ts_: s.activation(tmpA2[:, :], g2b[ic][:, ts_], AF.Copy,
                                                                scale=scales[:, 10:11], bias=MAGIC))
                W("vector", a1)
                cv = v_op(lambda v, ic=ic, ts_=ts_: v.tensor_scalar(
                    g2b[ic][:, ts_], tmpA2[:, :], -MAGIC, None, ALU.add))
                W("scalar", cv)
        oq_done = (sV, sV.n)
        W("tensor", oq_done)
        W("sync", oq_done)
        prev_w = None
        prev_bank = {}
        for fo in range(FC):
            if prev_w is not None:
                W("sync", prev_w)
            ld = prog.dma("sync", sQ, p5_w[:, :].rearrange("p (f c) -> p f c", c=P),
                          wdownq_sc.rearrange("(f p) o -> p f o", p=P)[:, :, fo * P:(fo + 1) * P])
            W("tensor", ld)
            for tc in range(NT):
                bank = pb[4 + (tc & 1)]
                if (tc & 1) in prev_bank:
                    W("tensor", prev_bank[tc & 1])
                for ic in range(IC):
                    mm(bank[:, :], p5_w[:, ic * P:(ic + 1) * P],
                       g2b[ic][:, tc * TCW:(tc + 1) * TCW],
                       start=(ic == 0), stop=(ic == IC - 1))
                md = t_mark()
                W("vector", md)
                res = v_op(lambda v, fo=fo, tc=tc, bank=bank: v.scalar_tensor_tensor(
                    x8[fo][:, tc * TCW:(tc + 1) * TCW], bank[:, :], scales[:, 11:12],
                    x8[fo][:, tc * TCW:(tc + 1) * TCW], ALU.mult, ALU.add))
                prev_bank[tc & 1] = res
            prev_w = (sT, sT.n)

    def rv(fn):
        return v_chain(fn)

    def derive_b1():
        rv(lambda v: v.tensor_scalar(row32[0:1, 16:17], row32[0:1, 8:9], FLOOR, None, ALU.max))
        rv(lambda v: v.reciprocal(row32[0:1, 17:18], row32[0:1, 16:17]))
        rv(lambda v: v.tensor_scalar(row32[0:1, 24:25], row32[0:1, 17:18], QMAX, None, ALU.mult))
        rv(lambda v: v.tensor_scalar(row32[0:1, 25:26], row32[0:1, 16:17], 1.0 / QMAX, None, ALU.mult))
        rv(lambda v: v.tensor_copy(row32[0:1, 18:19], row32[0:1, 25:26]))
        return replicate(24, 0, 2)

    def derive_b2():
        for (src, sc_c, rc_c, keep) in ((8, 24, 25, 19), (9, 26, 27, 20)):
            rv(lambda v, src=src: v.tensor_tensor(row32[0:1, 16:17], row32[0:1, src:src + 1],
                                                  row32[0:1, 18:19], ALU.mult))
            rv(lambda v: v.tensor_scalar(row32[0:1, 16:17], row32[0:1, 16:17], FLOOR, None, ALU.max))
            rv(lambda v, sc=sc_c: v.tensor_scalar(row32[0:1, sc:sc + 1], row32[0:1, 16:17],
                                                  1.0 / QMAX, None, ALU.mult))
            rv(lambda v: v.reciprocal(row32[0:1, 17:18], row32[0:1, 16:17]))
            rv(lambda v: v.tensor_scalar(row32[0:1, 17:18], row32[0:1, 17:18], QMAX, None, ALU.mult))
            rv(lambda v, rc=rc_c: v.tensor_tensor(row32[0:1, rc:rc + 1], row32[0:1, 17:18],
                                                  row32[0:1, 18:19], ALU.mult))
            rv(lambda v, sc=sc_c, keep=keep: v.tensor_copy(row32[0:1, keep:keep + 1],
                                                           row32[0:1, sc:sc + 1]))
        rv(lambda v: v.tensor_scalar(row32[0:1, 28:29], row32[0:1, 26:27], -MAGIC, None, ALU.mult))
        return replicate(24, 2, 5)

    def derive_b3():
        rv(lambda v: v.tensor_scalar(row32[0:1, 21:22], row32[0:1, 20:21], 1.0 / 65536.0, None, ALU.mult))
        rv(lambda v: v.tensor_tensor(row32[0:1, 16:17], row32[0:1, 8:9], row32[0:1, 21:22], ALU.mult))
        rv(lambda v: v.tensor_scalar(row32[0:1, 16:17], row32[0:1, 16:17], FLOOR, None, ALU.max))
        rv(lambda v: v.tensor_scalar(row32[0:1, 24:25], row32[0:1, 16:17], 1.0 / QMAX, None, ALU.mult))
        rv(lambda v: v.reciprocal(row32[0:1, 17:18], row32[0:1, 16:17]))
        rv(lambda v: v.tensor_scalar(row32[0:1, 17:18], row32[0:1, 17:18], QMAX, None, ALU.mult))
        rv(lambda v: v.tensor_tensor(row32[0:1, 25:26], row32[0:1, 17:18], row32[0:1, 21:22], ALU.mult))
        rv(lambda v: v.tensor_tensor(row32[0:1, 22:23], row32[0:1, 24:25], row32[0:1, 19:20], ALU.mult))
        return replicate(24, 8, 2)

    def derive_b4():
        rv(lambda v: v.tensor_tensor(row32[0:1, 16:17], row32[0:1, 8:9], row32[0:1, 22:23], ALU.mult))
        rv(lambda v: v.tensor_scalar(row32[0:1, 16:17], row32[0:1, 16:17], FLOOR, None, ALU.max))
        rv(lambda v: v.tensor_scalar(row32[0:1, 23:24], row32[0:1, 16:17], 1.0 / QMAX, None, ALU.mult))
        rv(lambda v: v.reciprocal(row32[0:1, 17:18], row32[0:1, 16:17]))
        rv(lambda v: v.tensor_scalar(row32[0:1, 17:18], row32[0:1, 17:18], QMAX, None, ALU.mult))
        rv(lambda v: v.tensor_tensor(row32[0:1, 24:25], row32[0:1, 17:18], row32[0:1, 22:23], ALU.mult))
        rv(lambda v: v.tensor_scalar(row32[0:1, 25:26], row32[0:1, 23:24], 0.5, None, ALU.mult))
        return replicate(24, 10, 2)

    def emit_ffn(call_idx):
        emit_ln(lambda i, t: xn8[i][:, t * TCW:(t + 1) * TCW], do_max=True,
                nt=(1 if phases == "ln1" else NT))
        if phases in ("lnx", "ln1"):
            sync_all()
            return
        colmax(0, NT * FC, 0)
        if phases == "cm":
            sync_all()
            return
        barrier_max()
        rep1 = derive_b1()
        W("scalar", rep1)
        if phases == "bar":
            sync_all()
            return
        for i in range(FC):
            a = s_op(lambda s, i=i: s.activation(xqi[:, :], xn8[i][:, :], AF.Copy,
                                                 scale=scales[:, 0:1], bias=MAGIC))
            W("vector", a)
            cv = v_op(lambda v, i=i: v.tensor_scalar(xq8[i][:, :], xqi[:, :], -MAGIC, None, ALU.add))
            W("scalar", cv)
        if phases == "ln":
            sync_all()
            return
        if call_idx == 0:
            sync_all()
            emit_dequant2(wup_n, IC, D, 32, wupq_sc)
            emit_dequant2(wgate_n, IC, D, 32, wgateq_sc)
            emit_dequant2(wdown_n, FC, I3, 96, wdownq_sc)
            dq_done = (sG, sG.n)
            W("sync", dq_done)
        sync_all()
        if phases in ("dq", "dqx"):
            return
        if phases == "p2a":
            emit_p2(1, 1)
            sync_all()
            return
        if phases == "p2b":
            emit_p2(1, IC)
            sync_all()
            return
        emit_p2()
        colmax(0, 96, 0)
        colmax(96, 96, 1)
        if phases in ("p2", "p2x", "p2y"):
            sync_all()
            return
        barrier_max()
        rep2 = derive_b2()
        W("scalar", rep2)
        emit_p3()
        if phases == "p3":
            sync_all()
            return
        colmax(0, 2 * IC, 0)
        barrier_max()
        rep3 = derive_b3()
        W("scalar", rep3)
        emit_p4()
        if phases == "p4":
            sync_all()
            return
        colmax(0, 2 * IC, 0)
        barrier_max()
        rep4 = derive_b4()
        W("scalar", rep4)
        emit_p5()
        sync_all()

    def emit_attention():
        emit_ln(lambda i, t: xn8[i][:, t * TCW:(t + 1) * TCW], do_max=False)
        ln2 = (sV, sV.n)
        W("gpsimd", ln2)
        for i in range(FC):
            prog.dma("gpsimd", sG, agx_in.ap()[i * P:(i + 1) * P, :], xn8[i][:, :])
        agw = (sG, sG.n)
        W("gpsimd", agw)
        agc = prog.inst("gpsimd", sC, lambda g: g.collective_compute(
            "AllGather", ALU.bypass, replica_groups=[[0, 1], [2, 3], [4, 5], [6, 7]],
            ins=[agx_in.ap()], outs=[agx_out.ap()]))
        W("sync", agw)
        prog.dma("sync", sQ, cos_t[:, :], costab[:, :])
        prog.dma("sync", sQ, sin_t[:, :], sintab[:, :])
        prev_cv = [None]
        for i in range(FC):
            if prev_cv[0] is not None:
                W("sync", prev_cv[0])
            ld = prog.dma("sync", sQ, wqkv_stage[:, :], wqkv_stage[:, :]) if False else \
                 prog.dma("sync", sQ, wqkv_stage[:, 0:1024], wqkvT[i * P:(i + 1) * P, 0:1024])
            W("vector", ld)
            cv = v_op(lambda v, i=i: v.tensor_copy(wqk_b[i][:, :], wqkv_stage[:, 0:1024]))
            prev_cv[0] = cv
        W("sync", agc)
        tabs = (sQ, sQ.n)
        W("tensor", tabs)
        W("vector", tabs)
        W("tensor", (sV, sV.n))

        rope_pairs = [(0, 2), (1, 3), (4, 6), (5, 7)]
        prev_eo = [None]
        prev_xs_use = {}
        for tc in range(NTA):
            contrib = 0 if tc < NT else 1
            ccols = slice((tc % NT) * TCW, (tc % NT + 1) * TCW)
            b = tc & 1
            if b in prev_xs_use:
                W("sync", prev_xs_use[b])
            ld = prog.dma("sync", sQ,
                          xs_big[b][:, :].rearrange("p (f c) -> p f c", c=TCW),
                          agx_out.ap()[contrib * D:(contrib + 1) * D, ccols]
                          .rearrange("(f p) c -> p f c", p=P))
            W("tensor", ld)
            acols = slice(tc * TCW, (tc + 1) * TCW)
            for pi, (eoc, ooc) in enumerate(rope_pairs):
                pE, pO = pb[(pi & 1) * 2], pb[(pi & 1) * 2 + 1]
                for f in range(FC):
                    mm(pE[:, :], wqk_b[f][:, eoc * P:(eoc + 1) * P],
                       xs_big[b][:, f * TCW:(f + 1) * TCW], start=(f == 0), stop=(f == FC - 1))
                for f in range(FC):
                    mm(pO[:, :], wqk_b[f][:, ooc * P:(ooc + 1) * P],
                       xs_big[b][:, f * TCW:(f + 1) * TCW], start=(f == 0), stop=(f == FC - 1))
                md = t_mark()
                W("vector", md)
                if prev_eo[0] is not None:
                    W("vector", prev_eo[0])
                v_op(lambda v, pE=pE, acols=acols: v.tensor_tensor(rope_t1[:, :], pE[:, :], cos_t[:, acols], ALU.mult))
                v_op(lambda v, pO=pO, acols=acols: v.tensor_tensor(rope_t2[:, :], pO[:, :], sin_t[:, acols], ALU.mult))
                v_op(lambda v: v.tensor_tensor(eo_stg[0][:, :], rope_t1[:, :], rope_t2[:, :], ALU.subtract))
                v_op(lambda v, pE=pE, acols=acols: v.tensor_tensor(rope_t1[:, :], pE[:, :], sin_t[:, acols], ALU.mult))
                v_op(lambda v, pO=pO, acols=acols: v.tensor_tensor(rope_t2[:, :], pO[:, :], cos_t[:, acols], ALU.mult))
                ve = v_op(lambda v: v.tensor_tensor(eo_stg[1][:, :], rope_t1[:, :], rope_t2[:, :], ALU.add))
                W("tensor", ve)
                W("gpsimd", ve)
                edst = qe_sc if pi < 2 else ke_sc
                odst = qo_sc if pi < 2 else ko_sc
                rr = 128 * (pi & 1)
                prog.dma("gpsimd", sG, edst[rr:rr + P, acols], eo_stg[0][:, :])
                d2 = prog.dma("gpsimd", sG, odst[rr:rr + P, acols], eo_stg[1][:, :])
                prev_eo[0] = d2
            prev_xs_use[b] = (sT, sT.n)
        rope_done_t = (sT, sT.n)
        rope_done_v = (sV, sV.n)
        W("sync", rope_done_t)
        W("sync", rope_done_v)
        prev_wv = [None]
        for i in range(FC):
            if prev_wv[0] is not None:
                W("sync", prev_wv[0])
            ldv = prog.dma("sync", sQ, tmpa[:, :], wqkvT[i * P:(i + 1) * P, 1024:1536])
            W("vector", ldv)
            cvv = v_op(lambda v, i=i: v.tensor_copy(wv_b[i][:, :], tmpa[:, :]))
            prev_wv[0] = cvv
        wv_done = (sV, sV.n)
        W("tensor", wv_done)
        prev_vb = {}
        for tc in range(NTA):
            contrib = 0 if tc < NT else 1
            ccols = slice((tc % NT) * TCW, (tc % NT + 1) * TCW)
            b = tc & 1
            if b in prev_xs_use:
                W("sync", prev_xs_use[b])
            ld = prog.dma("sync", sQ,
                          xs_big[b][:, :].rearrange("p (f c) -> p f c", c=TCW),
                          agx_out.ap()[contrib * D:(contrib + 1) * D, ccols]
                          .rearrange("(f p) c -> p f c", p=P))
            W("tensor", ld)
            for tk in range(4):
                bank = pb[4 + (tk & 1)]
                if (tk & 1) in prev_vb:
                    W("tensor", prev_vb[tk & 1])
                for f in range(FC):
                    mm(bank[:, :], xs_big[b][:, f * TCW + tk * P: f * TCW + tk * P + P],
                       wv_b[f][:, :], start=(f == 0), stop=(f == FC - 1))
                md = t_mark()
                W("scalar", md)
                vt = s_op(lambda s, tc=tc, tk=tk, bank=bank: s.activation(
                    v_all[tc * 4 + tk][:, :].rearrange("p (h c) -> p h c", c=65)[:, :, 0:64],
                    bank[:, :].rearrange("p (h c) -> p h c", c=64), AF.Copy))
                prev_vb[tk & 1] = vt
            prev_xs_use[b] = (sT, sT.n)
        for vt_i in range(32):
            v_op(lambda v, vt_i=vt_i: v.memset(
                v_all[vt_i][:, :].rearrange("p (h c) -> p h c", c=65)[:, :, 64:65], 1.0))
        vones = (sV, sV.n)
        eo_done = (sG, sG.n)
        W("sync", eo_done)
        W("sync", (sT, sT.n))
        W("sync", (sV, sV.n))
        W("sync", (sA, sA.n))
        for j in range(8):
            esrc = qe_sc if j < 4 else ke_sc
            osrc = qo_sc if j < 4 else ko_sc
            e0 = 64 * (j % 4)
            prog.dma("sync", sQ, qk_all[j][0:32, :], esrc[e0:e0 + 32, :])
            prog.dma("sync", sQ, qk_all[j][32:64, :], osrc[e0:e0 + 32, :])
            prog.dma("sync", sQ, qk_all[j][64:96, :], esrc[e0 + 32:e0 + 64, :])
            prog.dma("sync", sQ, qk_all[j][96:128, :], osrc[e0 + 32:e0 + 64, :])
        qk_ready = (sQ, sQ.n)
        W("scalar", qk_ready)
        W("tensor", qk_ready)
        W("vector", qk_ready)

        for src_k, base in ((0, 0), (4, 64)):
            for h in range(NH):
                tile = qk_all[src_k + h // 2]
                ro = 64 * (h & 1)
                for t4 in range(NTA):
                    cs = slice(t4 * TCW, (t4 + 1) * TCW)
                    sqd = s_op(lambda s, tile=tile, ro=ro, cs=cs: s.activation(
                        tmpa[ro:ro + 64, :], tile[ro:ro + 64, cs], AF.Square))
                    W("tensor", sqd)
                    mm(pb[6][0:1, :], onesc[ro:ro + 64, 0:1], tmpa[ro:ro + 64, :],
                       start=True, stop=True)
                    nm = t_mark()
                    W("vector", nm)
                    if src_k == 0:
                        vv = v_op(lambda v, h=h, t4=t4: v.tensor_reduce(
                            nq_row[0:1, h * 8 + t4:h * 8 + t4 + 1], pb[6][0:1, :], AX.X, ALU.max))
                    else:
                        vv = v_op(lambda v, h=h, t4=t4: v.tensor_reduce(
                            rowtmp[0:1, 64 + h:64 + h + 1] if t4 else nq_row[0:1, 64 + h:64 + h + 1],
                            pb[6][0:1, :], AX.X, ALU.max))
                        if t4:
                            vv = v_chain(lambda v, h=h: v.tensor_tensor(
                                nq_row[0:1, 64 + h:64 + h + 1], nq_row[0:1, 64 + h:64 + h + 1],
                                rowtmp[0:1, 64 + h:64 + h + 1], ALU.max))
                    W("tensor", vv)
                    W("scalar", vv)
        from concourse.bass import broadcast_tensor_aps
        for h in range(NH):
            a_ap, b_ap = broadcast_tensor_aps(nq_row[0:1, h * 8:h * 8 + 8],
                                              nq_row[0:1, 64 + h:64 + h + 1])
            v_chain(lambda v, a_ap=a_ap, b_ap=b_ap: v.tensor_tensor(
                a_ap, a_ap, b_ap, ALU.mult))
        nv = (sV, sV.n)
        W("scalar", nv)
        sq2 = s_op(lambda s: s.activation(nq_row[0:1, 0:64], nq_row[0:1, 0:64], AF.Sqrt))
        W("vector", sq2)
        v_chain(lambda v: v.tensor_scalar(nq_row[0:1, 0:64], nq_row[0:1, 0:64],
                                          -1.0 / 64.0, None, ALU.mult))
        bs = (sV, sV.n)
        W("gpsimd", bs)
        b1 = prog.dma("gpsimd", sG, brow_sc[0:1, 0:64], nq_row[0:1, 0:64])
        W("gpsimd", b1)
        b2 = prog.dma("gpsimd", sG, mrep[:, 0:64], brow_sc[0:1, 0:64].broadcast_to([P, 64]))
        W("scalar", b2)
        W("vector", vones)
        W("tensor", vones)

        last_exp_on_bank = {}
        prev_ao = [None]
        for h in range(NH):
            ktile = qk_all[4 + h // 2]
            qtile = qk_all[h // 2]
            ro = 64 * (h & 1)
            for qt in range(NTA):
                qcols = slice(qt * TCW, (qt + 1) * TCW)
                av_bank = pb[2 + (qt & 1)]
                nkc = 4 * qt + 4
                prev_avm = None
                for kc in range(nkc):
                    sbk = kc & 1
                    if sbk in last_exp_on_bank:
                        W("tensor", last_exp_on_bank[sbk])
                    mm(pb[sbk][:, :], ktile[ro:ro + 64, kc * P:(kc + 1) * P],
                       qtile[ro:ro + 64, qcols], start=True, stop=True)
                    md = t_mark()
                    W("scalar", md)
                    eb = kc & 1
                    if prev_avm is not None:
                        W("scalar", prev_avm)
                    col = h * 8 + qt
                    ex = s_op(lambda s, sbk=sbk, eb=eb, col=col: s.activation(
                        expS[eb][:, :], pb[sbk][:, :], AF.Exp, scale=0.125,
                        bias=mrep[:, col:col + 1]))
                    last_exp_on_bank[sbk] = ex
                    dep = ex
                    if kc >= 4 * qt:
                        off = kc - 4 * qt
                        W("vector", ex)
                        dep = v_op(lambda v, eb=eb, off=off: v.tensor_tensor(
                            expS[eb][:, :], expS[eb][:, :],
                            masks_t[:, off * TCW:(off + 1) * TCW], ALU.mult))
                        W("tensor", dep)
                        last_exp_on_bank[sbk] = dep
                    else:
                        W("tensor", ex)
                    mm(av_bank[0:65, :], v_all[kc][:, h * 65:(h + 1) * 65],
                       expS[eb][:, :], start=(kc == 0), stop=(kc == nkc - 1))
                    prev_avm = t_mark()
                W("vector", prev_avm)
                v_op(lambda v, av_bank=av_bank: v.reciprocal(tmpb[0:1, 0:TCW], av_bank[64:65, :]))
                rc = v_chain(lambda v: v.tensor_copy(recip_row[0:1, :], tmpb[0:1, 0:TCW]))
                W("tensor", rc)
                mm(pb[6][0:64, :], onesrb[0:1, 0:64], recip_row[0:1, :], start=True, stop=True)
                rep = t_mark()
                W("vector", rep)
                rcp = v_op(lambda v: v.tensor_copy(tmpa[0:64, :], pb[6][0:64, :]))
                W("tensor", rcp)
                if prev_ao[0] is not None:
                    W("vector", prev_ao[0])
                ao = v_op(lambda v, h=h, qcols=qcols, av_bank=av_bank: v.tensor_tensor(
                    ao_stg[64 * (h & 1):64 * (h & 1) + 64, qcols],
                    av_bank[0:64, :], tmpa[0:64, :], ALU.mult))
                W("tensor", ao)
            if h & 1:
                aod = (sV, sV.n)
                W("gpsimd", aod)
                dst_r = (h - 1) * 64
                ad = prog.dma("gpsimd", sG, aga_in.ap()[dst_r:dst_r + P, :], ao_stg[:, :])
                prev_ao[0] = ad
        agc2 = prog.inst("gpsimd", sC, lambda g: g.collective_compute(
            "AllGather", ALU.bypass, replica_groups=[[0, 1], [2, 3], [4, 5], [6, 7]],
            ins=[aga_in.ap()], outs=[aga_out.ap()]))

        av_all_done = (sT, sT.n)
        W("sync", av_all_done)
        prev_wo = [None]
        for k in range(16):
            wb = k & 1
            if prev_wo[0] is not None and k >= 2:
                W("sync", prev_wo[0])
            ld = prog.dma("sync", sQ, wo_raw[wb][:, :], woT4[k, :, :])
            W("vector", ld)
            cv = v_op(lambda v, k=k, wb=wb: v.tensor_copy(wo16[k][:, :], wo_raw[wb][:, :]))
            prev_wo[0] = cv
        W("sync", agc2)
        wo_done = (sV, sV.n)
        W("tensor", wo_done)
        prev_ob = {}
        for tcw in range(NT):
            for sl in range(4):
                half, contrib = sl // 2, sl % 2
                cols = slice(half * T + tcw * TCW, half * T + (tcw + 1) * TCW)
                for hf in range(4):
                    prog.dma("sync", sQ, agaS[sl * 4 + hf][:, :],
                             aga_out.ap()[contrib * 512 + hf * P: contrib * 512 + (hf + 1) * P, cols])
            agld = (sQ, sQ.n)
            W("tensor", agld)
            for fo in range(FC):
                bank = pb[6 + (fo & 1)]
                if (fo & 1) in prev_ob:
                    W("tensor", prev_ob[fo & 1])
                first = True
                for k in range(16):
                    mm(bank[:, :], wo16[k][:, fo * P:(fo + 1) * P], agaS[k][:, :],
                       start=first, stop=(k == 15))
                    first = False
                md = t_mark()
                W("vector", md)
                res = v_op(lambda v, fo=fo, tcw=tcw, bank=bank: v.tensor_tensor(
                    x8[fo][:, tcw * TCW:(tcw + 1) * TCW],
                    x8[fo][:, tcw * TCW:(tcw + 1) * TCW], bank[:, :], ALU.add))
                prev_ob[fo & 1] = res
            W("sync", (sT, sT.n))
        sync_all()

    if phases != "nop":
        emit_ffn(0)
    if phases in ("f1a", "full"):
        emit_attention()
    if phases == "full":
        emit_ffn(1)

    fin = (sV, sV.n)
    W("gpsimd", fin)
    for i in range(FC):
        prog.dma("gpsimd", sO, y_out[i * P:(i + 1) * P, :], x8[i][:, :])
    if phases == "p2x":
        prog.dma("gpsimd", sO, dbg_out[:, 0:96], mx[:, 0:96])
    elif phases == "p2y":
        prog.dma("gpsimd", sO, dbg_out[:, 0:96], mx[:, 96:192])
    else:
        prog.dma("gpsimd", sO, dbg_out[:, 0:64], scales[:, :])
        prog.dma("gpsimd", sO, dbg_out[0:1, 64:96], row32[:, :])

    from contextlib import ExitStack
    with ExitStack() as ctx:
        block = ctx.enter_context(nc.Block())

        @block.sync
        def _(eng):
            for fn in prog.q["sync"]:
                fn(eng)

        @block.tensor
        def _(eng):
            for fn in prog.q["tensor"]:
                fn(eng)

        @block.vector
        def _(eng):
            for fn in prog.q["vector"]:
                fn(eng)

        @block.scalar
        def _(eng):
            for fn in prog.q["scalar"]:
                fn(eng)

        @block.gpsimd
        def _(eng):
            for fn in prog.q["gpsimd"]:
                fn(eng)
            eng.wait_ge(sO.h, sO.n)

    return nc


def _host_inputs(inputs, core):
    pair, parity = core // 2, core & 1
    x = np.asarray(inputs["L0_attn_x"], np.float32)
    Wq = np.asarray(inputs["L0_attn_Wq"], np.float32)
    Wk = np.asarray(inputs["L0_attn_Wk"], np.float32)
    Wv = np.asarray(inputs["L0_attn_Wv"], np.float32)
    Wo = np.asarray(inputs["L0_attn_Wo"], np.float32)
    wg = np.asarray(inputs["L0_ffn_w_gate"], np.float32)
    wu = np.asarray(inputs["L0_ffn_w_up"], np.float32)
    wd = np.asarray(inputs["L0_ffn_w_down"], np.float32)

    xc = np.ascontiguousarray(x[pair, parity * T:(parity + 1) * T].T)

    rows = []
    for chunk in range(4):
        odd = 1 if chunk >= 2 else 0
        hbase = 4 * (chunk % 2)
        for hh in range(4):
            head = parity * 8 + hbase + hh
            for i in range(32):
                rows.append(head * 64 + 2 * i + odd)
    rows = np.array(rows)
    vrows = np.arange(parity * 512, (parity + 1) * 512)
    wqkvT = np.ascontiguousarray(
        np.concatenate([Wq[rows].T, Wk[rows].T, Wv[vrows].T], axis=1))

    WoT = np.ascontiguousarray(Wo.T)
    tiles = []
    for sl in range(4):
        half, contrib = sl // 2, sl % 2
        use = (half == parity)
        blk = WoT[contrib * 512:(contrib + 1) * 512] if use else np.zeros((512, D), np.float32)
        for hf in range(4):
            tiles.append(blk[hf * P:(hf + 1) * P])
    woT4 = np.ascontiguousarray(np.stack(tiles))

    inv = 10000.0 ** (-np.arange(32, dtype=np.float64) * 2.0 / DK)
    tpos = np.arange(TA, dtype=np.float32)
    ang = (tpos[None, :] * inv[:, None].astype(np.float32)).astype(np.float32)
    cos1 = np.cos(ang.astype(np.float64)).astype(np.float32)
    sin1 = np.sin(ang.astype(np.float64)).astype(np.float32)
    costab = np.ascontiguousarray(np.tile(cos1, (4, 1)))
    sintab = np.ascontiguousarray(np.tile(sin1, (4, 1)))

    kk = np.arange(P)[:, None]
    qq = np.arange(TCW)[None, :]
    m = np.concatenate([(kk + off * P <= qq) for off in range(4)], axis=1).astype(BFNP)

    return {
        "x_in": xc,
        "wqkvT": wqkvT,
        "woT4": woT4,
        "wup_n": np.ascontiguousarray(wu),
        "wgate_n": np.ascontiguousarray(wg),
        "wdown_n": np.ascontiguousarray(wd),
        "costab": costab,
        "sintab": sintab,
        "masks_in": np.ascontiguousarray(m),
        "onesc_in": np.concatenate([np.ones((P, 1), np.float32), np.full((P, 1), 1e-5, np.float32), np.full((P, 1), MAGIC, np.float32), np.ones((P, 5), np.float32)], axis=1),
        "onesr_in": np.ones((1, P), np.float32),
        "id_in": np.eye(P, dtype=np.float32),
    }


def kernel(**inputs):
    nc = build_program(_PHASES)
    in_maps = [_host_inputs(inputs, c) for c in range(8)]
    res = run_bass_kernel_spmd(nc, in_maps, core_ids=list(range(8)))
    out = np.zeros((4, TA, D), np.float32)
    for c in range(8):
        pair, parity = c // 2, c & 1
        out[pair, parity * T:(parity + 1) * T] = res.results[c]["y_out"].T
    return out
